# version 19
# baseline (speedup 1.0000x reference)
import numpy as np
import ml_dtypes
from contextlib import ExitStack
import concourse.bass as bass
import concourse.mybir as mybir
from concourse.bass_utils import run_bass_kernel_spmd

F32 = mybir.dt.float32
BF16 = mybir.dt.bfloat16
AF = mybir.ActivationFunctionType
ALU = mybir.AluOpType
AX = mybir.AxisListType

NCORES = 8
T = 4096
D = 1024
NC8 = 8
TA = 256
NTA = T // TA
TF = 1024
NQF = T // TF
ES = 128
FF = 32
EPS = 1e-5
PI = float(np.pi)


class Prog:
    ENGINES = ("sync", "tensor", "vector", "scalar", "gpsimd")

    def __init__(self, nc, stack):
        self.nc = nc
        self.ops = {e: [] for e in self.ENGINES}
        self.sems = {}
        self.cnt = {}
        for e in self.ENGINES:
            self.sems[e] = stack.enter_context(nc.semaphore("sem_" + e))
            self.cnt[e] = 0
        self.WR = {}
        self.LR = {}
        self._dma_cnt = {}
        self._last_tp = None

    DMA_ENGINES = ("sync", "gpsimd", "scalar")

    def emit(self, eng, fn, reads=(), writes=(), dma=False, transpose=False):
        waits = []
        for b in reads:
            waits += self.WR.get(b, [])
        for b in writes:
            waits += self.LR.pop(b, [])
            waits += self.WR.get(b, [])
        amt = 16 if dma else 1
        self.cnt[eng] += amt
        ev = (eng, self.cnt[eng])
        wmax = {}
        for (s, v) in waits:
            if wmax.get(s, -1) < v:
                wmax[s] = v
        self.ops[eng].append((tuple(wmax.items()), fn, amt))
        for b in reads:
            self.LR.setdefault(b, []).append(ev)
        for b in writes:
            self.WR[b] = [ev]
        return ev

    def barrier_event(self, bufs):
        evs = []
        for b in bufs:
            evs += self.WR.get(b, []) + self.LR.get(b, [])
        return evs

    def build(self):
        nc = self.nc
        with nc.Block() as block:
            for e in self.ENGINES:
                ops = self.ops[e]
                sems = self.sems
                mysem = sems[e]

                def body(engine, ops=ops, e=e):
                    last = {}
                    for waits, fn, amt in ops:
                        for (s, v) in waits:
                            if last.get(s, 0) >= v:
                                continue
                            engine.wait_ge(sems[s], v)
                            last[s] = v
                        inst = fn(engine)
                        inst.then_inc(mysem, amt)
                getattr(block, e)(body)


def ts_(i, n):
    return slice(i * n, (i + 1) * n)


def build_nc(debug_stop=None, f_stop=None):
    nc = bass.Bass("TRN2", target_bir_lowering=False)
    dt_in = {}

    def din(name, shape, dtype=F32):
        t = nc.dram_tensor(name, shape, dtype, kind="ExternalInput").ap()
        dt_in[name] = t
        return t

    x_h = din("x", [T, D])
    Wk_h = din("Wk", [D, D]); Wv_h = din("Wv", [D, D]); Wq_h = din("Wq", [D, D])
    Wkp_h = din("Wkp", [D, D]); Wqp_h = din("Wqp", [D, D]); Wo_h = din("Wo", [D, D])
    bk_h = din("bk", [128, 8]); bv_h = din("bv", [128, 8]); bq_h = din("bq", [128, 8])
    bkp_h = din("bkp", [128, 8]); bqp_h = din("bqp", [128, 8])
    ps_h = din("ps", [128, 8]); lng_h = din("lng", [128, 8]); lnb_h = din("lnb", [128, 8])
    bo_h = din("bo", [1, D])
    u64_h = din("u64", [128, 128])
    ident_h = din("ident", [128, 128])
    bvr_h = din("bvr", [128, D]); bkpr_h = din("bkpr", [128, D]); bqpr_h = din("bqpr", [128, D])
    psr_h = din("psr", [128, D])
    ctrl_h = din("ctrl", [D, ES])
    f1_h = din("f1", [NC8, 128, ES * FF])
    f2q_h = din("f2q", [128, 32 * D])
    biasq_h = din("biasq", [128, 32])
    f2q2_h = din("f2q2", [32, 128 * D])
    biasq2_h = din("biasq2", [32, 128])
    tie_h = din("tie", [128, TF])

    out_h = nc.dram_tensor("out", [T, D], F32, kind="ExternalOutput").ap()
    if debug_stop is not None:
        xmid_h = nc.dram_tensor("xmid", [T, D], F32, kind="ExternalOutput").ap()
        dbg_h = nc.dram_tensor("dbg", [T, D], F32, kind="ExternalOutput").ap()
    else:
        xmid_h = nc.dram_tensor("xmid", [T, D], F32, kind="Internal").ap()
        dbg_h = None
    f2bf_h = nc.dram_tensor("f2bf", [32, 128 * D], BF16, kind="Internal").ap()
    rl_h = nc.dram_tensor("rl", [32, 8192], BF16, kind="Internal").ap()

    with ExitStack() as stack:
        P = Prog(nc, stack)

        def sb(name, shape, dtype):
            return stack.enter_context(nc.sbuf_tensor(name, shape, dtype))


        a_stack = ExitStack()

        def sba(name, shape, dtype):
            return a_stack.enter_context(nc.sbuf_tensor(name, shape, dtype))

        W_sb = {n: sba("W_" + n, [128, 8192], BF16) for n in ("k", "v", "q", "kp", "qp", "o")}
        vec_sb = {n: sba("vec_" + n, [128, 8], F32)
                  for n in ("bk", "bv", "bq", "bkp", "bqp", "ps", "aps", "lng", "lnb")}
        u64_sb = sba("u64_sb", [128, 128], BF16)
        cpihalf = sba("cpihalf", [128, 1], F32)
        wtmp = [sba(f"wtmp{i}", [128, 1024], F32) for i in range(2)]
        wtmpb = sba("wtmpb", [128, 1024], BF16)
        bo_row = wtmp[0][0:1, 0:1024]

        xbuf = [sba(f"x{i}", [128, TA * 8], F32) for i in range(2)]
        xa = sba("xa", [128, TA * 8], F32)
        xn_tm = sba("xn_tm", [128, TA * 8], BF16)
        xn_fm = sba("xn_fm", [128, TA * 8], BF16)
        Kf = sba("Kf", [128, TA * 8], BF16)
        Qf = sba("Qf", [128, TA * 8], BF16)
        Vf = sba("Vf", [128, TA * 8], BF16)
        t0 = sba("t0", [128, TA * 8], F32)
        th = sba("th", [128, TA * 8], F32)
        sinb = sba("sinb", [128, TA * 8], BF16)
        cosb = sba("cosb", [128, TA * 8], BF16)
        br_tm = sba("br_tm", [128, TA * 8], BF16)
        bi_tm = sba("bi_tm", [128, TA * 8], BF16)
        rep = {n: sba("rep_" + n, [128, D], BF16) for n in ("bv", "bkp", "bqp", "ps")}
        ident_sb = sba("ident_sb", [128, 128], BF16)
        retr = sba("retr", [128, D], BF16)
        z_tm = sba("z_tm", [128, D], BF16)
        z_fm = sba("z_fm", [128, TA * 8], BF16)
        t_x = sba("t_x", [128, D], F32)
        r1 = th[:, 0:1024]
        r2 = t_x[:]
        bn6 = sba("bn6", [128, 12], F32)
        mv = sba("mv", [128, 4], F32)
        sc = sba("sc", [128, 8], F32)
        sc2 = sba("sc2", [128, 4], F32)
        bn6b2 = sba("bn6b2", [128, 12], F32)
        mv2 = sba("mv2", [128, 2], F32)
        psA = stack.enter_context(nc.psum_tensor("psA", [128, 2048], F32))
        psB = stack.enter_context(nc.psum_tensor("psB", [128, 2048], F32))

        I32 = mybir.dt.int32

        def emit_rsqrt(tv_ap, tv_name, out_ap, out_name, scr_ap, scr_name):
            ti = tv_ap.bitcast(I32)
            oi = out_ap.bitcast(I32)
            P.emit("vector", lambda v: v.tensor_single_scalar(oi, ti, 1, op=ALU.logical_shift_right),
                   reads=[tv_name], writes=[out_name])
            P.emit("vector", lambda v: v.tensor_scalar(oi, oi, -1, 0x5F3759DF, op0=ALU.mult, op1=ALU.add),
                   reads=[out_name], writes=[out_name])
            for _ in range(2):
                P.emit("vector", lambda v: v.tensor_mul(scr_ap, out_ap, out_ap),
                       reads=[out_name], writes=[scr_name])
                P.emit("vector", lambda v: v.tensor_scalar(scr_ap, scr_ap, -0.5, None, op0=ALU.mult),
                       reads=[scr_name], writes=[scr_name])
                P.emit("vector", lambda v: v.tensor_mul(scr_ap, scr_ap, tv_ap),
                       reads=[scr_name, tv_name], writes=[scr_name])
                P.emit("vector", lambda v: v.tensor_scalar_add(scr_ap, scr_ap, 1.5),
                       reads=[scr_name], writes=[scr_name])
                P.emit("vector", lambda v: v.tensor_mul(out_ap, out_ap, scr_ap),
                       reads=[scr_name, out_name], writes=[out_name])

        for n, h in (("k", Wk_h), ("v", Wv_h), ("q", Wq_h), ("kp", Wkp_h), ("qp", Wqp_h)):
            for c in range(NC8):
                buf = wtmp[c % 2]
                bname = f"wtmp{c % 2}"
                P.emit("gpsimd", lambda g, buf=buf, h=h, c=c: g.dma_start(buf[:], h[ts_(c, 128), :]),
                       writes=[bname], dma=True)
                P.emit("vector", lambda v, n=n, c=c, buf=buf: v.tensor_copy(
                    W_sb[n][:, ts_(c, 1024)], buf[:]),
                    reads=[bname], writes=["W_" + n])
        for n, h in (("bk", bk_h), ("bv", bv_h), ("bq", bq_h), ("bkp", bkp_h),
                     ("bqp", bqp_h), ("ps", ps_h), ("lng", lng_h), ("lnb", lnb_h)):
            P.emit("gpsimd", lambda s, n=n, h=h: s.dma_start(vec_sb[n][:], h),
                   writes=["vec_" + n], dma=True)
        for (tname, hsrc, dstap, dwr) in (
                ("u64", u64_h, u64_sb[:], "u64"), ("ident", ident_h, ident_sb[:], "ident")):
            P.emit("gpsimd", lambda g, hsrc=hsrc: g.dma_start(wtmp[0][:, 0:128], hsrc),
                   writes=["wtmp0"], dma=True)
            P.emit("vector", lambda v, dstap=dstap: v.tensor_copy(dstap, wtmp[0][:, 0:128]),
                   reads=["wtmp0"], writes=[dwr])
        for n, h in (("bv", bvr_h), ("bkp", bkpr_h), ("bqp", bqpr_h), ("ps", psr_h)):
            P.emit("gpsimd", lambda g, h=h: g.dma_start(wtmp[1][:], h),
                   writes=["wtmp1"], dma=True)
            P.emit("vector", lambda v, n=n: v.tensor_copy(rep[n][:], wtmp[1][:]),
                   reads=["wtmp1"], writes=["rep_" + n])
        P.emit("scalar", lambda s: s.activation(vec_sb["aps"][:], vec_sb["ps"][:], AF.Tanh),
               reads=["vec_ps"], writes=["vec_aps"])
        P.emit("scalar", lambda s: s.activation(vec_sb["aps"][:], vec_sb["ps"][:], AF.Abs),
               reads=["vec_ps"], writes=["vec_aps"])
        P.emit("vector", lambda v: v.memset(cpihalf[:], PI / 2), writes=["cpihalf"])
        for c in range(NC8):
            buf = wtmp[c % 2]
            bname = f"wtmp{c % 2}"
            src = Wo_h[ts_(c, 128), :]
            P.emit("gpsimd", lambda s, buf=buf, src=src: s.dma_start(buf[:], src),
                   reads=[], writes=[bname], dma=True)
            P.emit("vector", lambda v, c=c, buf=buf: v.tensor_scalar_mul(
                W_sb["o"][:, ts_(c, 1024)], buf[:], vec_sb["lng"][:, c:c + 1]),
                reads=[bname, "vec_lng"], writes=["W_o"])
            for h2 in range(2):
                first = (c == 0 and h2 == 0)
                last = (c == NC8 - 1 and h2 == 1)
                P.emit("tensor", lambda t, c=c, h2=h2, buf=buf: t.matmul(
                    psA[0:1, h2 * 512:(h2 + 1) * 512], vec_sb["lnb"][:, c:c + 1],
                    buf[:, h2 * 512:(h2 + 1) * 512], start=(c == 0), stop=(c == NC8 - 1)),
                    reads=[bname, "vec_lnb"], writes=["psA"] if (first or last) else [])
        P.emit("gpsimd", lambda s: s.dma_start(bo_row, bo_h), writes=["wtmp0"], dma=True)
        P.emit("vector", lambda v: v.tensor_add(bo_row, bo_row, psA[0:1, 0:1024]),
               reads=["psA", "wtmp0"], writes=["wtmp0"])
        ones1b = sba("ones1b", [1, 128], BF16)
        bo_bf = sba("bo_bf", [1, D], BF16)
        P.emit("vector", lambda v: v.memset(ones1b[:], 1.0), writes=["ones1b"])
        P.emit("vector", lambda v: v.tensor_copy(bo_bf[:], bo_row),
               reads=["wtmp0"], writes=["bo_bf"])


        def emit_precast_chunk(j):
            sl = slice(j * 1024, (j + 1) * 1024)
            P.emit("gpsimd", lambda g, sl=sl: g.dma_start(
                wtmp[1][0:32, :], f2q2_h[:, sl]),
                writes=["wtmp1"], dma=True)
            P.emit("vector", lambda v: v.tensor_copy(wtmpb[0:32, :], wtmp[1][0:32, :]),
                   reads=["wtmp1"], writes=["wtmpb"])
            P.emit("gpsimd", lambda g, sl=sl: g.dma_start(f2bf_h[:, sl], wtmpb[0:32, :]),
                   reads=["wtmpb"], writes=["f2bf"], dma=True)

        pA0 = psA[:, 0:1024]
        pA1 = psA[:, 1024:2048]

        def proj_fm(wname, in_fm, pslots, inbuf):
            in3 = in_fm.rearrange("p (u w) -> p u w", u=2)
            for j in range(8):
                for c in range(NC8):
                    first = (c == 0 and j == 0)
                    last = (c == NC8 - 1 and j == 7)
                    P.emit("tensor", lambda t, j=j, c=c, in3=in3: t.matmul(
                        psA[:, j * TA:(j + 1) * TA].rearrange("p (u w) -> p u w", u=2),
                        W_sb[wname][:, (c * 8 + j) * 128:(c * 8 + j) * 128 + 128],
                        in3[:, :, c * 128:(c + 1) * 128],
                        start=(c == 0), stop=(c == NC8 - 1)),
                        reads=["W_" + wname, inbuf],
                        writes=["pA0", "pA1"] if (first or last) else [])

        def evac_fm(biasvec, outbuf, outname):
            for j in range(8):
                P.emit("scalar", lambda sc_, j=j: sc_.add(
                    outbuf.rearrange("p (u w) -> p u w", u=2)[:, :, j * 128:(j + 1) * 128],
                    psA[:, j * TA:(j + 1) * TA].rearrange("p (u w) -> p u w", u=2),
                    vec_sb[biasvec][:, j:j + 1]),
                    reads=["pA0", "pA1", "vec_" + biasvec], writes=[outname])

        def proj_tm(wname, lhs_fm, lhsname):
            for u in range(2):
                for h in range(2):
                    for c in range(NC8):
                        first = (u == 0 and h == 0 and c == 0)
                        last = (u == 1 and h == 1 and c == NC8 - 1)
                        P.emit("tensor", lambda t, u=u, h=h, c=c: t.matmul(
                            psB[:, u * 1024 + h * 512: u * 1024 + h * 512 + 512],
                            lhs_fm[:, u * 1024 + c * 128: u * 1024 + c * 128 + 128],
                            W_sb[wname][:, (c * 8 + h * 4) * 128:(c * 8 + h * 4) * 128 + 512],
                            start=(c == 0), stop=(c == NC8 - 1)),
                            reads=["W_" + wname, lhsname],
                            writes=["pB"] if (first or last) else [])

        def evac_tm_rep(repname, outbuf, outname, dtype_note=None):
            for u in range(2):
                P.emit("vector", lambda v, u=u: v.tensor_add(
                    outbuf[:, ts_(u, 1024)], psB[:, ts_(u, 1024)], rep[repname][:]),
                    reads=["pB", "rep_" + repname], writes=[outname])

        def trig_tm():
            P.emit("scalar", lambda s: s.activation(th[:], t0[:], AF.Tanh),
                   reads=["t0"], writes=["th"])
            P.emit("vector", lambda v: v.tensor_mul(
                t0[:].rearrange("p (u w) -> p u w", u=2), th[:].rearrange("p (u w) -> p u w", u=2),
                rep["ps"][:].unsqueeze(1).broadcast_to([128, 2, 1024])),
                reads=["th", "rep_ps"], writes=["t0"])
            P.emit("scalar", lambda s: s.activation(sinb[:], t0[:], AF.Sin),
                   reads=["t0"], writes=["sinb"])
            P.emit("scalar", lambda s: s.activation(th[:], t0[:], AF.Abs),
                   reads=["t0"], writes=["th"])
            P.emit("scalar", lambda s: s.activation(cosb[:], th[:], AF.Sin,
                   bias=cpihalf[:], scale=-1.0),
                   reads=["th", "cpihalf"], writes=["cosb"])

        def transpose_pe(src_ap, srcname, dst_ap, dstname, slot_ap, slotname):
            slotb = slot_ap.bitcast(BF16)
            for c in range(NC8):
                P.emit("tensor", lambda t, c=c, slotb=slotb: t.transpose(
                    slotb[:, c * 128:(c + 1) * 128], src_ap[:, c * 128:(c + 1) * 128],
                    ident_sb[:]),
                    reads=[srcname, "ident"],
                    writes=[slotname] if c in (0, NC8 - 1) else [])
            P.emit("vector", lambda v, slotb=slotb: v.tensor_copy(dst_ap, slotb[:, 0:1024]),
                   reads=[slotname], writes=[dstname])

        for i in range(NTA):
            xb = xbuf[i % 2]
            xname = f"x{i % 2}"
            P.emit("gpsimd", lambda s, i=i, xb=xb: s.dma_start(
                xb[:].rearrange("p (a q) -> p a q", q=1024),
                x_h[ts_(i, TA), :].rearrange("(a p) q -> p a q", p=128)),
                writes=[xname], dma=True)

            for u in range(2):
                P.emit("vector", lambda v, u=u, xb=xb: v.bn_stats(
                    bn6[:, 0:6], xb[:, u * 1024:u * 1024 + 512]),
                    reads=[xname], writes=["bn6a"])
                P.emit("vector", lambda v, u=u, xb=xb: v.bn_stats(
                    bn6[:, 6:12], xb[:, u * 1024 + 512:u * 1024 + 1024]),
                    reads=[xname], writes=["bn6b"])
                P.emit("vector", lambda v, u=u: v.bn_aggr(mv[:, 2 * u:2 * u + 2], bn6[:]),
                       reads=["bn6a", "bn6b"], writes=["mv"])
                P.emit("vector", lambda v, u=u: v.tensor_scalar_add(
                    sc[:, 0:1], mv[:, 2 * u + 1:2 * u + 2], EPS), reads=["mv"], writes=["sc0"])
                emit_rsqrt(sc[:, 0:1], "sc0", sc[:, 2 + u:3 + u], f"rstd{u}", sc[:, 1:2], "sc1")
                P.emit("vector", lambda v, u=u: v.tensor_scalar(
                    sc[:, 4 + u:5 + u], mv[:, 2 * u:2 * u + 1], sc[:, 2 + u:3 + u], -1.0,
                    op0=ALU.mult, op1=ALU.mult), reads=["mv", f"rstd{u}"], writes=[f"nmr{u}"])
                P.emit("vector", lambda v, u=u: v.tensor_scalar_add(
                    sc[:, 6 + u:7 + u], sc[:, 2 + u:3 + u], 1.0),
                    reads=[f"rstd{u}"], writes=[f"s1_{u}"])
                P.emit("vector", lambda v, u=u, xb=xb: v.tensor_scalar(
                    xn_tm[:, ts_(u, 1024)], xb[:, ts_(u, 1024)],
                    mv[:, 2 * u:2 * u + 1], sc[:, 2 + u:3 + u],
                    op0=ALU.subtract, op1=ALU.mult),
                    reads=[xname, "mv", f"rstd{u}"], writes=["xn_tm"])
                P.emit("vector", lambda v, u=u, xb=xb: v.tensor_scalar(
                    xa[:, ts_(u, 1024)], xb[:, ts_(u, 1024)], sc[:, 6 + u:7 + u],
                    sc[:, 4 + u:5 + u], op0=ALU.mult, op1=ALU.add),
                    reads=[xname, f"s1_{u}", f"nmr{u}"], writes=["xa"])
                transpose_pe(xn_tm[:, ts_(u, 1024)], "xn_tm",
                             xn_fm[:, ts_(u, 1024)], "xn_fm",
                             psB[:, ts_(u, 1024)], "pB")

            proj_fm("k", xn_fm, None, "xn_fm")
            evac_fm("bk", Kf, "Kf")
            proj_tm("v", xn_fm, "xn_fm")
            evac_tm_rep("bv", Vf, "Vf")
            proj_fm("q", xn_fm, None, "xn_fm")
            evac_fm("bq", Qf, "Qf")

            proj_tm("kp", Kf, "Kf")
            evac_tm_rep("bkp", t0, "t0")
            trig_tm()
            P.emit("vector", lambda v: v.tensor_mul(br_tm[:], Vf[:], cosb[:]),
                   reads=["Vf", "cosb"], writes=["br_tm"])
            P.emit("vector", lambda v: v.tensor_mul(bi_tm[:], Vf[:], sinb[:]),
                   reads=["Vf", "sinb"], writes=["bi_tm"])

            proj_tm("qp", Qf, "Qf")
            evac_tm_rep("bqp", t0, "t0")
            trig_tm()

            for u in range(2):
                for (srcb, srcn, off) in ((br_tm, "br_tm", 0), (bi_tm, "bi_tm", 1024)):
                    for h2 in range(2):
                        first = (off == 0 and h2 == 0)
                        last = (off == 1024 and h2 == 1)
                        P.emit("tensor", lambda t, u=u, srcb=srcb, off=off, h2=h2: t.matmul(
                            psB[:, off + h2 * 512: off + h2 * 512 + 512], u64_sb[:],
                            srcb[:, u * 1024 + h2 * 512: u * 1024 + h2 * 512 + 512],
                            start=True, stop=True),
                            reads=["u64", srcn],
                            writes=["pB"] if (first or last) else [])
                P.emit("vector", lambda v, u=u: v.tensor_mul(
                    r1, psB[:, 0:1024], cosb[:, ts_(u, 1024)]),
                    reads=["pB", "cosb"], writes=["th"])
                P.emit("vector", lambda v, u=u: v.tensor_mul(
                    r2, psB[:, 1024:2048], sinb[:, ts_(u, 1024)]),
                    reads=["pB", "sinb"], writes=["t_x"])
                P.emit("vector", lambda v, u=u: v.tensor_add(
                    retr[:], r1, r2),
                    reads=["th", "t_x"], writes=["retr"])
                P.emit("vector", lambda v: v.bn_stats(bn6b2[:, 0:6], retr[:, 0:512]),
                       reads=["retr"], writes=["bn6a2"])
                P.emit("vector", lambda v: v.bn_stats(bn6b2[:, 6:12], retr[:, 512:1024]),
                       reads=["retr"], writes=["bn6b2"])
                P.emit("vector", lambda v: v.bn_aggr(mv2[:, 0:2], bn6b2[:]),
                       reads=["bn6a2", "bn6b2"], writes=["mv2"])
                P.emit("vector", lambda v: v.tensor_scalar_add(sc2[:, 0:1], mv2[:, 1:2], EPS),
                       reads=["mv2"], writes=["sc20"])
                emit_rsqrt(sc2[:, 0:1], "sc20", sc2[:, 2:3], "rstd2", sc2[:, 1:2], "sc21")
                P.emit("vector", lambda v: v.tensor_scalar(
                    z_tm[:], retr[:], mv2[:, 0:1], sc2[:, 2:3],
                    op0=ALU.subtract, op1=ALU.mult),
                    reads=["retr", "mv2", "rstd2"], writes=["z_tm"])
                transpose_pe(z_tm[:], "z_tm", z_fm[:, ts_(u, 1024)], "z_fm", pA0, "pA0")
                for h2 in range(2):
                    for c in range(NC8):
                        first = (h2 == 0 and c == 0)
                        P.emit("tensor", lambda t, u=u, h2=h2, c=c: t.matmul(
                            pA1[:, h2 * 512:h2 * 512 + 512],
                            z_fm[:, u * 1024 + c * 128: u * 1024 + c * 128 + 128],
                            W_sb["o"][:, (c * 8 + h2 * 4) * 128:(c * 8 + h2 * 4) * 128 + 512],
                            start=(c == 0), stop=False),
                            reads=["z_fm", "W_o"],
                            writes=["pA1"] if first else [])
                    P.emit("tensor", lambda t, h2=h2: t.matmul(
                        pA1[:, h2 * 512:h2 * 512 + 512], ones1b[:],
                        bo_bf[:, h2 * 512:(h2 + 1) * 512], start=False, stop=True),
                        reads=["ones1b", "bo_bf"],
                        writes=["pA1"] if h2 == 1 else [])
                P.emit("vector", lambda v, u=u: v.scalar_tensor_tensor(
                    t_x[:], pA1[:, 0:1024], 1.0, xa[:, ts_(u, 1024)],
                    op0=ALU.mult, op1=ALU.add),
                    reads=["pA1", "xa"], writes=["t_x"])
                P.emit("gpsimd", lambda s, i=i, u=u: s.dma_start(
                    xmid_h[i * TA + u * 128: i * TA + (u + 1) * 128, :], t_x[:]),
                    reads=["t_x"], writes=["xmid_done"], dma=True)
            for jj in range(8):
                emit_precast_chunk(i * 8 + jj)

        if debug_stop == "wdump":
            wdump_h = nc.dram_tensor("wdump", [6, 128, 8192], BF16, kind="ExternalOutput").ap()
            for k, n in enumerate(("k", "v", "q", "kp", "qp", "o")):
                P.emit("gpsimd", lambda g, k=k, n=n: g.dma_start(wdump_h[k], W_sb[n][:]),
                       reads=["W_" + n, "xmid_done"], writes=["wdump_done"], dma=True)
        if debug_stop is not None:
            P.emit("gpsimd", lambda g: g.engine_nop(), reads=["xmid_done", "wdump_done"] if debug_stop == "wdump" else ["xmid_done"])
            a_stack.close()
            P.build()
            return nc

        snap = dict(P.cnt)
        for e in P.ENGINES:
            waits = tuple((o, snap[o]) for o in P.ENGINES if o != e and snap[o] > 0)
            P.cnt[e] += 1
            P.ops[e].append((waits, lambda eng: eng.nop(), 1))
        a_stack.close()

        f_stack = ExitStack()

        def sbf(name, shape, dtype):
            return f_stack.enter_context(nc.sbuf_tensor(name, shape, dtype))

        f1_sb = sbf("f1_sb", [128, 8 * ES * FF], BF16)
        D_sb = sbf("D_sb", [128, 8 * ES * 32], BF16)
        Ir = sbf("Ir", [128, 32 * 512], BF16)
        hi_tm = sbf("hi_tm", [128, 8192], BF16)
        lo_ts = sbf("lo_ts", [128, D], BF16)
        t1f = sbf("t1f", [128, D], F32)
        xbf = [sbf(f"xbf{i}", [128, D], F32) for i in range(2)]
        tie_sb = sbf("tie_sb", [128, TF], BF16)
        mb = sbf("mb", [128, 32], F32)
        perm_fm = sbf("perm_fm", [128, TF], BF16)
        perm_tm = sbf("perm_tm", [128, 8 * ES], BF16)
        chi = sbf("chi", [128, 8 * 128], BF16)
        clo = sbf("clo", [128, 8 * 128], BF16)
        biasq2 = sbf("biasq2_sb", [32, 128], F32)
        fbn6 = sbf("fbn6", [128, 12], F32)
        fmv = sbf("fmv", [128, 2], F32)
        fsc = sbf("fsc", [128, 4], F32)
        fident = sbf("fident", [128, 128], BF16)
        pexp = sbf("pexp", [128, 512], BF16)
        hi_fm = D_sb[:, 0:8192]
        lo_fm = D_sb[:, 8192:16384]
        logit_sb = D_sb[:, 16384:18432].bitcast(F32)
        innerf = D_sb[0:32, 16384:24576].bitcast(F32)
        f2blk = [D_sb[0:32, 8192:16384], D_sb[0:32, 16384:24576]]
        stgbig = D_sb[0:32, 24576:32768]
        f2st = D_sb[0:32, 4096:12288].bitcast(F32)
        inner = D_sb[0:32, 0:4096]
        sctf = t1f[0:32, 0:0 + 1024]

        P.emit("vector", lambda v: v.memset(pexp[:], 0.0), writes=["pexp"])
        P.emit("gpsimd", lambda g: g.dma_start(xbf[0][:, 0:128], ident_h),
               writes=["xbf0"], dma=True)
        P.emit("vector", lambda v: v.tensor_copy(fident[:], xbf[0][:, 0:128]),
               reads=["xbf0"], writes=["fident"])
        for c in range(NC8):
            for j in range(4):
                sl = slice(j * 1024, (j + 1) * 1024)
                P.emit("gpsimd", lambda g, c=c, sl=sl: g.dma_start(
                    xbf[1][:], f1_h[c, :, sl]), writes=["xbf1"], dma=True)
                P.emit("vector", lambda v, c=c, sl=sl: v.tensor_copy(
                    f1_sb[:, c * 4096 + sl.start: c * 4096 + sl.stop], xbf[1][:]),
                    reads=["xbf1"], writes=["f1_sb"])
        for c in range(NC8):
            P.emit("gpsimd", lambda g, c=c: g.dma_start(
                xbf[0][:, 0:128], ctrl_h[ts_(c, 128), :]), writes=["xbf0"], dma=True)
            P.emit("vector", lambda v, c=c: v.tensor_copy(
                chi[:, ts_(c, 128)], xbf[0][:, 0:128]), reads=["xbf0"], writes=["chi"])
            P.emit("vector", lambda v, c=c: v.tensor_sub(
                clo[:, ts_(c, 128)], xbf[0][:, 0:128], chi[:, ts_(c, 128)]),
                reads=["xbf0", "chi"], writes=["clo"])
        P.emit("gpsimd", lambda g: g.dma_start(biasq2[:], biasq2_h),
               writes=["biasq"], dma=True)
        P.emit("gpsimd", lambda g: g.dma_start(xbf[0][:], tie_h[:, 0:1024]),
               writes=["xbf0"], dma=True)
        P.emit("vector", lambda v: v.tensor_copy(tie_sb[:], xbf[0][:]),
               reads=["xbf0"], writes=["tie_sb"])

        def transpose_f(src_ap, srcname, dst_ap, dstname, slot_ap, slotname):
            slotb = slot_ap.bitcast(BF16)
            for c in range(NC8):
                P.emit("tensor", lambda t, c=c, slotb=slotb, src_ap=src_ap: t.transpose(
                    slotb[:, c * 128:(c + 1) * 128], src_ap[:, c * 128:(c + 1) * 128],
                    fident[:]),
                    reads=[srcname, "fident"],
                    writes=[slotname] if c in (0, NC8 - 1) else [])
            P.emit("vector", lambda v, slotb=slotb, dst_ap=dst_ap: v.tensor_copy(
                dst_ap, slotb[:, 0:1024]),
                reads=[slotname], writes=[dstname])

        NQ_EMIT = 1 if f_stop else NQF
        for qq in range(NQ_EMIT):
            for w in range(8):
                xb = xbf[w % 2]
                xbn = f"xbf{w % 2}"
                row0 = qq * TF + w * 128
                P.emit("gpsimd", lambda g, row0=row0, xb=xb: g.dma_start(
                    xb[:], xmid_h[row0:row0 + 128, :]),
                    reads=["xmid_done"], writes=[xbn], dma=True)
                P.emit("gpsimd", lambda g, row0=row0, xb=xb: g.dma_start(
                    out_h[row0:row0 + 128, :], xb[:]),
                    reads=[xbn], writes=["out_pre"], dma=True)
                P.emit("vector", lambda v, xb=xb: v.bn_stats(fbn6[:, 0:6], xb[:, 0:512]),
                       reads=[xbn], writes=["fbn6a"])
                P.emit("vector", lambda v, xb=xb: v.bn_stats(fbn6[:, 6:12], xb[:, 512:1024]),
                       reads=[xbn], writes=["fbn6b"])
                P.emit("vector", lambda v: v.bn_aggr(fmv[:], fbn6[:]),
                       reads=["fbn6a", "fbn6b"], writes=["fmv"])
                P.emit("vector", lambda v: v.tensor_scalar_add(fsc[:, 0:1], fmv[:, 1:2], EPS),
                       reads=["fmv"], writes=["fsc0"])
                emit_rsqrt(fsc[:, 0:1], "fsc0", fsc[:, 2:3], "frstd", fsc[:, 1:2], "fsc1")
                P.emit("vector", lambda v, xb=xb: v.tensor_scalar(
                    t1f[:], xb[:], fmv[:, 0:1], fsc[:, 2:3],
                    op0=ALU.subtract, op1=ALU.mult),
                    reads=[xbn, "fmv", "frstd"], writes=["t1f"])
                P.emit("vector", lambda v, w=w: v.tensor_copy(
                    hi_tm[:, ts_(w, 1024)], t1f[:]),
                    reads=["t1f"], writes=["hi_tm"])
                P.emit("vector", lambda v, w=w: v.tensor_sub(
                    lo_ts[:], t1f[:], hi_tm[:, ts_(w, 1024)]),
                    reads=["t1f", "hi_tm"], writes=["lo_ts"])
                transpose_f(hi_tm[:, ts_(w, 1024)], "hi_tm",
                            hi_fm[:, ts_(w, 1024)], "D", pA0, "pA0")
                transpose_f(lo_ts[:], "lo_ts",
                            lo_fm[:, ts_(w, 1024)], "D", pA0, "pA0")

            if f_stop == "ln3":
                break
            terms = ((chi, "chi", hi_fm), (clo, "clo", hi_fm), (chi, "chi", lo_fm))
            for w in range(8):
                k = 0
                for (cw, cwn, act) in terms:
                    for c in range(NC8):
                        first = (k == 0)
                        last = (k == 3 * NC8 - 1)
                        P.emit("tensor", lambda t, w=w, c=c, cw=cw, act=act, first=first, last=last: t.matmul(
                            psB[:, w * 128:(w + 1) * 128],
                            cw[:, ts_(c, 128)],
                            act[:, w * 1024 + c * 128: w * 1024 + c * 128 + 128],
                            start=first, stop=last),
                            reads=[cwn, "D"],
                            writes=["pB"] if ((first and w == 0) or (last and w == 7)) else [])
                        k += 1
            P.emit("vector", lambda v: v.tensor_add(logit_sb, psB[:, 0:1024], tie_sb[:]),
                   reads=["pB", "tie_sb"], writes=["D"])
            P.emit("vector", lambda v: v.reduce_max(
                mb[:], logit_sb.rearrange("p (a b) -> p a b", b=32), axis=AX.X),
                reads=["D"], writes=["mb"])
            P.emit("vector", lambda v: v.tensor_tensor(
                perm_fm[:].rearrange("p (a b) -> p a b", b=32),
                logit_sb.rearrange("p (a b) -> p a b", b=32),
                mb[:].unsqueeze(2).broadcast_to([128, 32, 32]), op=ALU.is_equal),
                reads=["D", "mb"], writes=["perm_fm"])
            slotb = pA0.bitcast(BF16)
            for w in range(8):
                P.emit("tensor", lambda t, w=w, slotb=slotb: t.transpose(
                    slotb[:, w * 128:(w + 1) * 128], perm_fm[:, w * 128:(w + 1) * 128],
                    fident[:]),
                    reads=["perm_fm", "fident"],
                    writes=["pA0"] if w in (0, 7) else [])
            P.emit("vector", lambda v, slotb=slotb: v.tensor_copy(perm_tm[:], slotb[:, 0:1024]),
                   reads=["pA0"], writes=["perm_tm"])

            if f_stop == "perm":
                break
            for w in range(8):
                for gi in range(4):
                    P.emit("vector", lambda v, w=w, gi=gi: v.tensor_copy(
                        pexp[32 * gi:32 * gi + 32, gi * 128:(gi + 1) * 128],
                        perm_tm[32 * gi:32 * gi + 32, w * 128:(w + 1) * 128]),
                        reads=["perm_tm"], writes=["pexp"])
                for c in range(NC8):
                    slot = pA1 if ((w * 8 + c) % 2) else pA0
                    slotn = "pA1" if ((w * 8 + c) % 2) else "pA0"
                    P.emit("tensor", lambda t, w=w, c=c, slot=slot: t.matmul(
                        slot[:, 0:512],
                        hi_tm[:, w * 1024 + c * 128: w * 1024 + (c + 1) * 128],
                        pexp[:],
                        start=True, stop=True),
                        reads=["hi_tm", "pexp"],
                        writes=[slotn])
                    src = slot[:, 0:512].rearrange("p (g e) -> p e g", g=4)
                    dst = D_sb[:, c * 4096:(c + 1) * 4096].rearrange(
                        "p (e g) -> p e g", g=32)[:, :, 4 * w:4 * w + 4]
                    P.emit("scalar", lambda sc_, src=src, dst=dst: sc_.copy(dst, src),
                           reads=[slotn], writes=["D"])

            if f_stop == "dispatch":
                break
            def f1_region(e):
                if e < 32:
                    return pA0[0:32, e * 32:(e + 1) * 32], "pA0"
                if e < 64:
                    return pA1[0:32, (e - 64 + 32) * 32:(e - 32 + 1) * 32] if False else pA1[0:32, (e - 32) * 32:(e - 31) * 32], "pA1"
                return psB[0:32, (e - 64) * 32:(e - 63) * 32], "pB"
            for e in range(ES):
                reg, regn = f1_region(e)
                for c in range(NC8):
                    first = (c == 0 and e in (0, 32, 64))
                    last = (c == NC8 - 1 and e in (31, 63, 127))
                    P.emit("tensor", lambda t, e=e, c=c, reg=reg: t.matmul(
                        reg,
                        f1_sb[:, (c * ES + e) * FF:(c * ES + e) * FF + FF],
                        D_sb[:, c * 4096 + e * 32: c * 4096 + e * 32 + 32],
                        start=(c == 0), stop=(c == NC8 - 1)),
                        reads=["f1_sb", "D"],
                        writes=[regn] if (first or last) else [])
            for (reg, regn, e0, ne) in ((pA0[0:32, :], "pA0", 0, 32), (pA1[0:32, :], "pA1", 32, 32), (psB[0:32, 0:2048], "pB", 64, 64)):
                bb = biasq2[:, e0:e0 + ne].unsqueeze(2).broadcast_to([32, ne, 32])
                P.emit("vector", lambda v, reg=reg, e0=e0, ne=ne, bb=bb: v.tensor_tensor(
                    innerf[:, 0:ne * 32].rearrange("p (e g) -> p e g", g=32),
                    reg.rearrange("p (e g) -> p e g", g=32), bb, op=ALU.add),
                    reads=[regn, "biasq"], writes=["D"])
                P.emit("vector", lambda v, e0=e0, ne=ne: v.tensor_scalar_max(
                    inner[:, e0 * 32:(e0 + ne) * 32], innerf[:, 0:ne * 32], 0.0),
                    reads=["D"], writes=["D"])

            if f_stop == "f1":
                break
            for pb_ in (32, 64, 96):
                P.emit("vector", lambda v, pb_=pb_: v.memset(D_sb[pb_:pb_ + 32, 0:4096], 0.0),
                       reads=[], writes=["Din", "D"])
                P.emit("vector", lambda v, pb_=pb_: v.memset(D_sb[pb_:pb_ + 32, 8192:24576], 0.0),
                       reads=[], writes=["Df2a", "Df2b", "D"])
            for hd in range(2):
                for eb in range(8):
                    fb = f2blk[eb % 2]
                    fbF = D_sb[:, 8192 + (eb % 2) * 8192: 8192 + (eb % 2) * 8192 + 8192]
                    fbn_ = "Df2a" if (eb % 2 == 0) else "Df2b"
                    innerF = D_sb[:, 0:4096]
                    P.emit("gpsimd", lambda g, eb=eb, hd=hd, fbF=fbF: g.dma_start(
                        fbF[0:32, :].rearrange("p (e d) -> p e d", d=512),
                        f2bf_h[:, eb * 16384:(eb + 1) * 16384].rearrange(
                            "p (e d) -> p e d", d=1024)[:, :, hd * 512:hd * 512 + 512]),
                        reads=["f2bf"], writes=[fbn_, "D"], dma=True)
                    for ei in range(16):
                        e = eb * 16 + ei
                        slot = pA0 if (e % 2 == 0) else pA1
                        slotn = "pA0" if (e % 2 == 0) else "pA1"
                        P.emit("tensor", lambda t, e=e, ei=ei, fbF=fbF, slot=slot: t.matmul(
                            slot[0:32, 0:512],
                            innerF[:, e * 32:(e + 1) * 32],
                            fbF[:, ei * 512:(ei + 1) * 512],
                            start=True, stop=True),
                            reads=["Din", fbn_],
                            writes=[slotn])
                        P.emit("scalar", lambda sc_, ei=ei, slot=slot: sc_.copy(
                            stgbig[:, ei * 512:(ei + 1) * 512], slot[0:32, 0:512]),
                            reads=[slotn], writes=["Dstg"])
                    P.emit("gpsimd", lambda g: g.dma_start(rl_h, stgbig[:, 0:8192]),
                           reads=["Dstg"], writes=["rl"], dma=True)
                    src = rl_h.rearrange("g (e d) -> e g d", d=512)
                    dst = Ir[eb * 16:(eb + 1) * 16, :].rearrange("p (g d) -> p g d", d=512)
                    P.emit("gpsimd", lambda g, src=src, dst=dst: g.dma_start(dst, src),
                           reads=["rl"], writes=["Ir"], dma=True)
                if f_stop == "f2":
                    continue
                for w in range(8):
                    for gi in range(4):
                        gg = 4 * w + gi
                        P.emit("tensor", lambda t, w=w, gi=gi, gg=gg: t.matmul(
                            psB[0:32, gi * 512:(gi + 1) * 512],
                            perm_fm[:, w * 128 + 32 * gi: w * 128 + 32 * gi + 32],
                            Ir[:, gg * 512:(gg + 1) * 512],
                            start=True, stop=True),
                            reads=["perm_fm", "Ir"],
                            writes=["pB"] if gi in (0, 3) else [])
                    P.emit("vector", lambda v: v.tensor_copy(t1f[0:32, 0:1024], psB[0:32, 0:1024]),
                           reads=["pB"], writes=["t1f"])
                    P.emit("vector", lambda v: v.tensor_copy(t1f[0:32, 0:0 + 1024], t1f[0:32, 0:1024]) if False else v.tensor_copy(lo_ts[0:32, 0:1024].bitcast(F32)[:, 0:512], psB[0:32, 1024:1536]),
                           reads=["pB"], writes=["lo_ts"])
                    P.emit("vector", lambda v: v.tensor_copy(lo_ts[0:32, 0:1024].bitcast(F32)[:, 512:1024] if False else xbf[0][0:32, 512:1024], psB[0:32, 1536:2048]),
                           reads=["pB"], writes=["xbf0"])
                    row0 = qq * TF + w * 128
                    P.emit("gpsimd", lambda g, row0=row0, hd=hd: g.dma_start(
                        out_h[row0:row0 + 64, hd * 512:hd * 512 + 512].rearrange(
                            "(g p) d -> p g d", p=32),
                        t1f[0:32, 0:1024].rearrange("p (g d) -> p g d", d=512),
                        accum_op=ALU.add),
                        reads=["t1f", "out_pre"], writes=["out_done"], dma=True)
                    P.emit("gpsimd", lambda g, row0=row0, hd=hd: g.dma_start(
                        out_h[row0 + 64:row0 + 96, hd * 512:hd * 512 + 512],
                        lo_ts[0:32, 0:1024].bitcast(F32)[:, 0:512],
                        accum_op=ALU.add),
                        reads=["lo_ts", "out_pre"], writes=["out_done"], dma=True)
                    P.emit("gpsimd", lambda g, row0=row0, hd=hd: g.dma_start(
                        out_h[row0 + 96:row0 + 128, hd * 512:hd * 512 + 512],
                        xbf[0][0:32, 512:1024],
                        accum_op=ALU.add),
                        reads=["xbf0", "out_pre"], writes=["out_done"], dma=True)

        if f_stop is None:
            P.emit("gpsimd", lambda g: g.nop(), reads=["out_done"])
        else:
            allb = [b for b in ("out_done", "Ir", "D", "perm_tm", "hi_tm") if b in P.WR]
            for b in allb:
                P.emit("gpsimd", lambda g: g.nop(), reads=[b])
        f_stack.close()
        P.build()
        return nc


def _host_inputs(inp):
    x = np.ascontiguousarray(np.asarray(inp["L0_attn_x"], dtype=np.float32).reshape(16 * 2048, 1024))
    def v8(a):
        return np.ascontiguousarray(np.asarray(a, dtype=np.float32).reshape(8, 128).T)
    base = {
        "Wk": np.ascontiguousarray(inp["L0_attn_Wk"], dtype=np.float32),
        "Wv": np.ascontiguousarray(inp["L0_attn_Wv"], dtype=np.float32),
        "Wq": np.ascontiguousarray(inp["L0_attn_Wq"], dtype=np.float32),
        "Wkp": np.ascontiguousarray(inp["L0_attn_Wkp"], dtype=np.float32),
        "Wqp": np.ascontiguousarray(inp["L0_attn_Wqp"], dtype=np.float32),
        "Wo": np.ascontiguousarray(inp["L0_attn_Wo"], dtype=np.float32),
        "bk": v8(inp["L0_attn_bk"]), "bv": v8(inp["L0_attn_bv"]), "bq": v8(inp["L0_attn_bq"]),
        "bkp": v8(inp["L0_attn_bkp"]), "bqp": v8(inp["L0_attn_bqp"]),
        "ps": v8(inp["L0_attn_phase_scale"]),
        "lng": v8(inp["L0_attn_ln_g"]), "lnb": v8(inp["L0_attn_ln_b"]),
        "bo": np.ascontiguousarray(np.asarray(inp["L0_attn_bo"], dtype=np.float32).reshape(1, 1024)),
    }
    k = np.arange(128)
    u64 = ((k[:, None] // 64 == k[None, :] // 64) & (k[:, None] <= k[None, :]))
    base["u64"] = u64.astype(np.float32)
    base["ident"] = np.eye(128, dtype=np.float32)
    base["bvr"] = np.ascontiguousarray(np.broadcast_to(
        np.asarray(inp["L0_attn_bv"], dtype=np.float32)[None, :], (128, 1024)))
    base["bkpr"] = np.ascontiguousarray(np.broadcast_to(
        np.asarray(inp["L0_attn_bkp"], dtype=np.float32)[None, :], (128, 1024)))
    base["bqpr"] = np.ascontiguousarray(np.broadcast_to(
        np.asarray(inp["L0_attn_bqp"], dtype=np.float32)[None, :], (128, 1024)))
    base["psr"] = np.ascontiguousarray(np.broadcast_to(
        np.asarray(inp["L0_attn_phase_scale"], dtype=np.float32)[None, :], (128, 1024)))
    tie = np.linspace(0.0, 1e-6, 32, dtype=np.float32)
    base["tie"] = np.ascontiguousarray(np.broadcast_to(np.tile(tie, TF // 32)[None, :], (128, TF)))
    ctrl = np.ascontiguousarray(np.asarray(inp["L0_ffn_controller"], dtype=np.float32).reshape(1024, 128))
    f1 = np.ascontiguousarray(np.asarray(inp["L0_ffn_f1"], dtype=np.float32).reshape(8, 128, 128 * 32))
    fb = np.asarray(inp["L0_ffn_bias"], dtype=np.float32).reshape(128, 32)
    f2 = np.asarray(inp["L0_ffn_f2"], dtype=np.float32).reshape(128, 32, 1024)
    e = np.arange(128)
    r_, q_ = e // 32, e % 32
    biasq = np.zeros((128, 32), dtype=np.float32)
    f2q = np.zeros((128, 32 * 1024), dtype=np.float32)
    for ei in range(128):
        rr, qq = ei // 32, ei % 32
        biasq[32 * rr: 32 * rr + 32, qq] = fb[ei]
        f2q[32 * rr: 32 * rr + 32, qq * 1024:(qq + 1) * 1024] = f2[ei]
    base["ctrl"] = ctrl
    base["f1"] = f1
    base["biasq"] = np.ascontiguousarray(biasq)
    base["f2q"] = np.ascontiguousarray(f2q)
    base["biasq2"] = np.ascontiguousarray(fb.T)
    base["f2q2"] = np.ascontiguousarray(np.transpose(f2, (1, 0, 2)).reshape(32, 128 * 1024))

    in_maps = []
    for i in range(8):
        m = dict(base)
        m["x"] = np.ascontiguousarray(x[i * T:(i + 1) * T])
        in_maps.append(m)
    return in_maps


_DEBUG_STOP = None


def kernel(**inputs):
    in_maps = _host_inputs(inputs)
    nc = build_nc(debug_stop=_DEBUG_STOP)
    res = run_bass_kernel_spmd(nc, in_maps, core_ids=list(range(NCORES)))
    key = "xmid" if _DEBUG_STOP == "attn" else "out"
    out = np.concatenate([np.asarray(r[key]) for r in res.results], axis=0)
    return out.reshape(16, 2048, 1024)


# version 20
# speedup vs baseline: 1.0454x; 1.0454x over previous
import numpy as np
import ml_dtypes
from contextlib import ExitStack
import concourse.bass as bass
import concourse.mybir as mybir
from concourse.bass_utils import run_bass_kernel_spmd

F32 = mybir.dt.float32
BF16 = mybir.dt.bfloat16
AF = mybir.ActivationFunctionType
ALU = mybir.AluOpType
AX = mybir.AxisListType

NCORES = 8
T = 4096
D = 1024
NC8 = 8
TA = 256
NTA = T // TA
TF = 1024
NQF = T // TF
ES = 128
FF = 32
EPS = 1e-5
PI = float(np.pi)


class Prog:
    ENGINES = ("sync", "tensor", "vector", "scalar", "gpsimd")

    def __init__(self, nc, stack):
        self.nc = nc
        self.ops = {e: [] for e in self.ENGINES}
        self.sems = {}
        self.cnt = {}
        for e in self.ENGINES:
            self.sems[e] = stack.enter_context(nc.semaphore("sem_" + e))
            self.cnt[e] = 0
        self.WR = {}
        self.LR = {}
        self._dma_cnt = {}
        self._last_tp = None

    DMA_ENGINES = ("sync", "gpsimd", "scalar")

    def emit(self, eng, fn, reads=(), writes=(), dma=False, transpose=False):
        waits = []
        for b in reads:
            waits += self.WR.get(b, [])
        for b in writes:
            waits += self.LR.pop(b, [])
            waits += self.WR.get(b, [])
        amt = 16 if dma else 1
        self.cnt[eng] += amt
        ev = (eng, self.cnt[eng])
        wmax = {}
        for (s, v) in waits:
            if wmax.get(s, -1) < v:
                wmax[s] = v
        self.ops[eng].append((tuple(wmax.items()), fn, amt))
        for b in reads:
            self.LR.setdefault(b, []).append(ev)
        for b in writes:
            self.WR[b] = [ev]
        return ev

    def barrier_event(self, bufs):
        evs = []
        for b in bufs:
            evs += self.WR.get(b, []) + self.LR.get(b, [])
        return evs

    def build(self):
        nc = self.nc
        with nc.Block() as block:
            for e in self.ENGINES:
                ops = self.ops[e]
                sems = self.sems
                mysem = sems[e]

                def body(engine, ops=ops, e=e):
                    last = {}
                    for waits, fn, amt in ops:
                        for (s, v) in waits:
                            if last.get(s, 0) >= v:
                                continue
                            engine.wait_ge(sems[s], v)
                            last[s] = v
                        inst = fn(engine)
                        inst.then_inc(mysem, amt)
                getattr(block, e)(body)


def ts_(i, n):
    return slice(i * n, (i + 1) * n)


def build_nc(debug_stop=None, f_stop=None):
    nc = bass.Bass("TRN2", target_bir_lowering=False)
    dt_in = {}

    def din(name, shape, dtype=F32):
        t = nc.dram_tensor(name, shape, dtype, kind="ExternalInput").ap()
        dt_in[name] = t
        return t

    x_h = din("x", [T, D])
    Wk_h = din("Wk", [D, D]); Wv_h = din("Wv", [D, D]); Wq_h = din("Wq", [D, D])
    Wkp_h = din("Wkp", [D, D]); Wqp_h = din("Wqp", [D, D]); Wo_h = din("Wo", [D, D])
    bk_h = din("bk", [128, 8]); bv_h = din("bv", [128, 8]); bq_h = din("bq", [128, 8])
    bkp_h = din("bkp", [128, 8]); bqp_h = din("bqp", [128, 8])
    ps_h = din("ps", [128, 8]); lng_h = din("lng", [128, 8]); lnb_h = din("lnb", [128, 8])
    bo_h = din("bo", [1, D])
    u64_h = din("u64", [128, 128])
    ident_h = din("ident", [128, 128])
    bvr_h = din("bvr", [128, D]); bkpr_h = din("bkpr", [128, D]); bqpr_h = din("bqpr", [128, D])
    psr_h = din("psr", [128, D])
    ctrl_h = din("ctrl", [D, ES])
    f1_h = din("f1", [NC8, 128, ES * FF])
    f2q_h = din("f2q", [128, 32 * D])
    biasq_h = din("biasq", [128, 32])
    f2q2_h = din("f2q2", [32, 128 * D])
    biasq2_h = din("biasq2", [32, 128])
    tie_h = din("tie", [128, TF])

    out_h = nc.dram_tensor("out", [T, D], F32, kind="ExternalOutput").ap()
    if debug_stop is not None:
        xmid_h = nc.dram_tensor("xmid", [T, D], F32, kind="ExternalOutput").ap()
        dbg_h = nc.dram_tensor("dbg", [T, D], F32, kind="ExternalOutput").ap()
    else:
        xmid_h = nc.dram_tensor("xmid", [T, D], F32, kind="Internal").ap()
        dbg_h = None
    f2bf_h = nc.dram_tensor("f2bf", [32, 128 * D], BF16, kind="Internal").ap()
    rl_h = nc.dram_tensor("rl", [32, 8192], BF16, kind="Internal").ap()

    with ExitStack() as stack:
        P = Prog(nc, stack)

        def sb(name, shape, dtype):
            return stack.enter_context(nc.sbuf_tensor(name, shape, dtype))


        a_stack = ExitStack()

        def sba(name, shape, dtype):
            return a_stack.enter_context(nc.sbuf_tensor(name, shape, dtype))

        W_sb = {n: sba("W_" + n, [128, 8192], BF16) for n in ("k", "v", "q", "kp", "qp", "o")}
        vec_sb = {n: sba("vec_" + n, [128, 8], F32)
                  for n in ("bk", "bv", "bq", "bkp", "bqp", "ps", "aps", "lng", "lnb")}
        u64_sb = sba("u64_sb", [128, 128], BF16)
        cpihalf = sba("cpihalf", [128, 1], F32)
        wtmp = [sba(f"wtmp{i}", [128, 1024], F32) for i in range(2)]
        wtmpb = sba("wtmpb", [128, 1024], BF16)
        bo_row = wtmp[0][0:1, 0:1024]

        xbuf = [sba(f"x{i}", [128, TA * 8], F32) for i in range(2)]
        xa = sba("xa", [128, TA * 8], F32)
        xn_tm = sba("xn_tm", [128, TA * 8], BF16)
        xn_fm = sba("xn_fm", [128, TA * 8], BF16)
        Kf = sba("Kf", [128, TA * 8], BF16)
        Qf = sba("Qf", [128, TA * 8], BF16)
        Vf = sba("Vf", [128, TA * 8], BF16)
        t0 = sba("t0", [128, TA * 8], F32)
        th = sba("th", [128, TA * 8], F32)
        sinb = sba("sinb", [128, TA * 8], BF16)
        cosb = sba("cosb", [128, TA * 8], BF16)
        br_tm = sba("br_tm", [128, TA * 8], BF16)
        bi_tm = sba("bi_tm", [128, TA * 8], BF16)
        rep = {n: sba("rep_" + n, [128, D], BF16) for n in ("bv", "bkp", "bqp", "ps")}
        ident_sb = sba("ident_sb", [128, 128], BF16)
        retr = sba("retr", [128, D], BF16)
        z_tm = sba("z_tm", [128, D], BF16)
        z_fm = sba("z_fm", [128, TA * 8], BF16)
        t_x = sba("t_x", [128, D], F32)
        r1 = th[:, 0:1024]
        r2 = t_x[:]
        bn6 = sba("bn6", [128, 12], F32)
        mv = sba("mv", [128, 4], F32)
        sc = sba("sc", [128, 8], F32)
        sc2 = sba("sc2", [128, 4], F32)
        bn6b2 = sba("bn6b2", [128, 12], F32)
        mv2 = sba("mv2", [128, 2], F32)
        psA = stack.enter_context(nc.psum_tensor("psA", [128, 2048], F32))
        psB = stack.enter_context(nc.psum_tensor("psB", [128, 2048], F32))

        I32 = mybir.dt.int32

        def emit_rsqrt(tv_ap, tv_name, out_ap, out_name, scr_ap, scr_name):
            ti = tv_ap.bitcast(I32)
            oi = out_ap.bitcast(I32)
            P.emit("vector", lambda v: v.tensor_single_scalar(oi, ti, 1, op=ALU.logical_shift_right),
                   reads=[tv_name], writes=[out_name])
            P.emit("vector", lambda v: v.tensor_scalar(oi, oi, -1, 0x5F3759DF, op0=ALU.mult, op1=ALU.add),
                   reads=[out_name], writes=[out_name])
            for _ in range(2):
                P.emit("vector", lambda v: v.tensor_mul(scr_ap, out_ap, out_ap),
                       reads=[out_name], writes=[scr_name])
                P.emit("vector", lambda v: v.tensor_scalar(scr_ap, scr_ap, -0.5, None, op0=ALU.mult),
                       reads=[scr_name], writes=[scr_name])
                P.emit("vector", lambda v: v.tensor_mul(scr_ap, scr_ap, tv_ap),
                       reads=[scr_name, tv_name], writes=[scr_name])
                P.emit("vector", lambda v: v.tensor_scalar_add(scr_ap, scr_ap, 1.5),
                       reads=[scr_name], writes=[scr_name])
                P.emit("vector", lambda v: v.tensor_mul(out_ap, out_ap, scr_ap),
                       reads=[scr_name, out_name], writes=[out_name])

        for n, h in (("k", Wk_h), ("v", Wv_h), ("q", Wq_h), ("kp", Wkp_h), ("qp", Wqp_h)):
            for c in range(NC8):
                buf = wtmp[c % 2]
                bname = f"wtmp{c % 2}"
                P.emit("gpsimd", lambda g, buf=buf, h=h, c=c: g.dma_start(buf[:], h[ts_(c, 128), :]),
                       writes=[bname], dma=True)
                P.emit("vector", lambda v, n=n, c=c, buf=buf: v.tensor_copy(
                    W_sb[n][:, ts_(c, 1024)], buf[:]),
                    reads=[bname], writes=["W_" + n])
        for n, h in (("bk", bk_h), ("bv", bv_h), ("bq", bq_h), ("bkp", bkp_h),
                     ("bqp", bqp_h), ("ps", ps_h), ("lng", lng_h), ("lnb", lnb_h)):
            P.emit("gpsimd", lambda s, n=n, h=h: s.dma_start(vec_sb[n][:], h),
                   writes=["vec_" + n], dma=True)
        for (tname, hsrc, dstap, dwr) in (
                ("u64", u64_h, u64_sb[:], "u64"), ("ident", ident_h, ident_sb[:], "ident")):
            P.emit("gpsimd", lambda g, hsrc=hsrc: g.dma_start(wtmp[0][:, 0:128], hsrc),
                   writes=["wtmp0"], dma=True)
            P.emit("vector", lambda v, dstap=dstap: v.tensor_copy(dstap, wtmp[0][:, 0:128]),
                   reads=["wtmp0"], writes=[dwr])
        for n, h in (("bv", bvr_h), ("bkp", bkpr_h), ("bqp", bqpr_h), ("ps", psr_h)):
            P.emit("gpsimd", lambda g, h=h: g.dma_start(wtmp[1][:], h),
                   writes=["wtmp1"], dma=True)
            P.emit("vector", lambda v, n=n: v.tensor_copy(rep[n][:], wtmp[1][:]),
                   reads=["wtmp1"], writes=["rep_" + n])
        P.emit("scalar", lambda s: s.activation(vec_sb["aps"][:], vec_sb["ps"][:], AF.Tanh),
               reads=["vec_ps"], writes=["vec_aps"])
        P.emit("scalar", lambda s: s.activation(vec_sb["aps"][:], vec_sb["ps"][:], AF.Abs),
               reads=["vec_ps"], writes=["vec_aps"])
        P.emit("vector", lambda v: v.memset(cpihalf[:], PI / 2), writes=["cpihalf"])
        for c in range(NC8):
            buf = wtmp[c % 2]
            bname = f"wtmp{c % 2}"
            src = Wo_h[ts_(c, 128), :]
            P.emit("gpsimd", lambda s, buf=buf, src=src: s.dma_start(buf[:], src),
                   reads=[], writes=[bname], dma=True)
            P.emit("vector", lambda v, c=c, buf=buf: v.tensor_scalar_mul(
                W_sb["o"][:, ts_(c, 1024)], buf[:], vec_sb["lng"][:, c:c + 1]),
                reads=[bname, "vec_lng"], writes=["W_o"])
            for h2 in range(2):
                first = (c == 0 and h2 == 0)
                last = (c == NC8 - 1 and h2 == 1)
                P.emit("tensor", lambda t, c=c, h2=h2, buf=buf: t.matmul(
                    psA[0:1, h2 * 512:(h2 + 1) * 512], vec_sb["lnb"][:, c:c + 1],
                    buf[:, h2 * 512:(h2 + 1) * 512], start=(c == 0), stop=(c == NC8 - 1)),
                    reads=[bname, "vec_lnb"], writes=["psA"] if (first or last) else [])
        P.emit("gpsimd", lambda s: s.dma_start(bo_row, bo_h), writes=["wtmp0"], dma=True)
        P.emit("vector", lambda v: v.tensor_add(bo_row, bo_row, psA[0:1, 0:1024]),
               reads=["psA", "wtmp0"], writes=["wtmp0"])
        ones1b = sba("ones1b", [1, 128], BF16)
        bo_bf = sba("bo_bf", [1, D], BF16)
        P.emit("vector", lambda v: v.memset(ones1b[:], 1.0), writes=["ones1b"])
        P.emit("vector", lambda v: v.tensor_copy(bo_bf[:], bo_row),
               reads=["wtmp0"], writes=["bo_bf"])


        pA0 = psA[:, 0:1024]
        pA1 = psA[:, 1024:2048]

        def proj_fm(wname, in_fm, pslots, inbuf):
            in3 = in_fm.rearrange("p (u w) -> p u w", u=2)
            for j in range(8):
                for c in range(NC8):
                    first = (c == 0 and j == 0)
                    last = (c == NC8 - 1 and j == 7)
                    P.emit("tensor", lambda t, j=j, c=c, in3=in3: t.matmul(
                        psA[:, j * TA:(j + 1) * TA].rearrange("p (u w) -> p u w", u=2),
                        W_sb[wname][:, (c * 8 + j) * 128:(c * 8 + j) * 128 + 128],
                        in3[:, :, c * 128:(c + 1) * 128],
                        start=(c == 0), stop=(c == NC8 - 1)),
                        reads=["W_" + wname, inbuf],
                        writes=["pA0", "pA1"] if (first or last) else [])

        def evac_fm(biasvec, outbuf, outname):
            for j in range(8):
                P.emit("scalar", lambda sc_, j=j: sc_.add(
                    outbuf.rearrange("p (u w) -> p u w", u=2)[:, :, j * 128:(j + 1) * 128],
                    psA[:, j * TA:(j + 1) * TA].rearrange("p (u w) -> p u w", u=2),
                    vec_sb[biasvec][:, j:j + 1]),
                    reads=["pA0", "pA1", "vec_" + biasvec], writes=[outname])

        def proj_tm(wname, lhs_fm, lhsname):
            for u in range(2):
                for h in range(2):
                    for c in range(NC8):
                        first = (u == 0 and h == 0 and c == 0)
                        last = (u == 1 and h == 1 and c == NC8 - 1)
                        P.emit("tensor", lambda t, u=u, h=h, c=c: t.matmul(
                            psB[:, u * 1024 + h * 512: u * 1024 + h * 512 + 512],
                            lhs_fm[:, u * 1024 + c * 128: u * 1024 + c * 128 + 128],
                            W_sb[wname][:, (c * 8 + h * 4) * 128:(c * 8 + h * 4) * 128 + 512],
                            start=(c == 0), stop=(c == NC8 - 1)),
                            reads=["W_" + wname, lhsname],
                            writes=["pB"] if (first or last) else [])

        def evac_tm_rep(repname, outbuf, outname, dtype_note=None):
            for u in range(2):
                P.emit("vector", lambda v, u=u: v.tensor_add(
                    outbuf[:, ts_(u, 1024)], psB[:, ts_(u, 1024)], rep[repname][:]),
                    reads=["pB", "rep_" + repname], writes=[outname])

        def trig_tm():
            P.emit("scalar", lambda s: s.activation(th[:], t0[:], AF.Tanh),
                   reads=["t0"], writes=["th"])
            P.emit("vector", lambda v: v.tensor_mul(
                t0[:].rearrange("p (u w) -> p u w", u=2), th[:].rearrange("p (u w) -> p u w", u=2),
                rep["ps"][:].unsqueeze(1).broadcast_to([128, 2, 1024])),
                reads=["th", "rep_ps"], writes=["t0"])
            P.emit("scalar", lambda s: s.activation(sinb[:], t0[:], AF.Sin),
                   reads=["t0"], writes=["sinb"])
            P.emit("scalar", lambda s: s.activation(th[:], t0[:], AF.Abs),
                   reads=["t0"], writes=["th"])
            P.emit("scalar", lambda s: s.activation(cosb[:], th[:], AF.Sin,
                   bias=cpihalf[:], scale=-1.0),
                   reads=["th", "cpihalf"], writes=["cosb"])

        def transpose_pe(src_ap, srcname, dst_ap, dstname, slot_ap, slotname):
            slotb = slot_ap.bitcast(BF16)
            for c in range(NC8):
                P.emit("tensor", lambda t, c=c, slotb=slotb: t.transpose(
                    slotb[:, c * 128:(c + 1) * 128], src_ap[:, c * 128:(c + 1) * 128],
                    ident_sb[:]),
                    reads=[srcname, "ident"],
                    writes=[slotname] if c in (0, NC8 - 1) else [])
            P.emit("vector", lambda v, slotb=slotb: v.tensor_copy(dst_ap, slotb[:, 0:1024]),
                   reads=[slotname], writes=[dstname])

        for i in range(NTA):
            xb = xbuf[i % 2]
            xname = f"x{i % 2}"
            P.emit("gpsimd", lambda s, i=i, xb=xb: s.dma_start(
                xb[:].rearrange("p (a q) -> p a q", q=1024),
                x_h[ts_(i, TA), :].rearrange("(a p) q -> p a q", p=128)),
                writes=[xname], dma=True)

            for u in range(2):
                P.emit("vector", lambda v, u=u, xb=xb: v.bn_stats(
                    bn6[:, 0:6], xb[:, u * 1024:u * 1024 + 512]),
                    reads=[xname], writes=["bn6a"])
                P.emit("vector", lambda v, u=u, xb=xb: v.bn_stats(
                    bn6[:, 6:12], xb[:, u * 1024 + 512:u * 1024 + 1024]),
                    reads=[xname], writes=["bn6b"])
                P.emit("vector", lambda v, u=u: v.bn_aggr(mv[:, 2 * u:2 * u + 2], bn6[:]),
                       reads=["bn6a", "bn6b"], writes=["mv"])
                P.emit("vector", lambda v, u=u: v.tensor_scalar_add(
                    sc[:, 0:1], mv[:, 2 * u + 1:2 * u + 2], EPS), reads=["mv"], writes=["sc0"])
                emit_rsqrt(sc[:, 0:1], "sc0", sc[:, 2 + u:3 + u], f"rstd{u}", sc[:, 1:2], "sc1")
                P.emit("vector", lambda v, u=u: v.tensor_scalar(
                    sc[:, 4 + u:5 + u], mv[:, 2 * u:2 * u + 1], sc[:, 2 + u:3 + u], -1.0,
                    op0=ALU.mult, op1=ALU.mult), reads=["mv", f"rstd{u}"], writes=[f"nmr{u}"])
                P.emit("vector", lambda v, u=u: v.tensor_scalar_add(
                    sc[:, 6 + u:7 + u], sc[:, 2 + u:3 + u], 1.0),
                    reads=[f"rstd{u}"], writes=[f"s1_{u}"])
                P.emit("vector", lambda v, u=u, xb=xb: v.tensor_scalar(
                    xn_tm[:, ts_(u, 1024)], xb[:, ts_(u, 1024)],
                    mv[:, 2 * u:2 * u + 1], sc[:, 2 + u:3 + u],
                    op0=ALU.subtract, op1=ALU.mult),
                    reads=[xname, "mv", f"rstd{u}"], writes=["xn_tm"])
                P.emit("vector", lambda v, u=u, xb=xb: v.tensor_scalar(
                    xa[:, ts_(u, 1024)], xb[:, ts_(u, 1024)], sc[:, 6 + u:7 + u],
                    sc[:, 4 + u:5 + u], op0=ALU.mult, op1=ALU.add),
                    reads=[xname, f"s1_{u}", f"nmr{u}"], writes=["xa"])
                transpose_pe(xn_tm[:, ts_(u, 1024)], "xn_tm",
                             xn_fm[:, ts_(u, 1024)], "xn_fm",
                             psB[:, ts_(u, 1024)], "pB")

            proj_fm("k", xn_fm, None, "xn_fm")
            evac_fm("bk", Kf, "Kf")
            proj_tm("v", xn_fm, "xn_fm")
            evac_tm_rep("bv", Vf, "Vf")
            proj_fm("q", xn_fm, None, "xn_fm")
            evac_fm("bq", Qf, "Qf")

            proj_tm("kp", Kf, "Kf")
            evac_tm_rep("bkp", t0, "t0")
            trig_tm()
            P.emit("vector", lambda v: v.tensor_mul(br_tm[:], Vf[:], cosb[:]),
                   reads=["Vf", "cosb"], writes=["br_tm"])
            P.emit("vector", lambda v: v.tensor_mul(bi_tm[:], Vf[:], sinb[:]),
                   reads=["Vf", "sinb"], writes=["bi_tm"])

            proj_tm("qp", Qf, "Qf")
            evac_tm_rep("bqp", t0, "t0")
            trig_tm()

            for u in range(2):
                for (srcb, srcn, off) in ((br_tm, "br_tm", 0), (bi_tm, "bi_tm", 1024)):
                    for h2 in range(2):
                        first = (off == 0 and h2 == 0)
                        last = (off == 1024 and h2 == 1)
                        P.emit("tensor", lambda t, u=u, srcb=srcb, off=off, h2=h2: t.matmul(
                            psB[:, off + h2 * 512: off + h2 * 512 + 512], u64_sb[:],
                            srcb[:, u * 1024 + h2 * 512: u * 1024 + h2 * 512 + 512],
                            start=True, stop=True),
                            reads=["u64", srcn],
                            writes=["pB"] if (first or last) else [])
                P.emit("vector", lambda v, u=u: v.tensor_mul(
                    r1, psB[:, 0:1024], cosb[:, ts_(u, 1024)]),
                    reads=["pB", "cosb"], writes=["th"])
                P.emit("vector", lambda v, u=u: v.tensor_mul(
                    r2, psB[:, 1024:2048], sinb[:, ts_(u, 1024)]),
                    reads=["pB", "sinb"], writes=["t_x"])
                P.emit("vector", lambda v, u=u: v.tensor_add(
                    retr[:], r1, r2),
                    reads=["th", "t_x"], writes=["retr"])
                P.emit("vector", lambda v: v.bn_stats(bn6b2[:, 0:6], retr[:, 0:512]),
                       reads=["retr"], writes=["bn6a2"])
                P.emit("vector", lambda v: v.bn_stats(bn6b2[:, 6:12], retr[:, 512:1024]),
                       reads=["retr"], writes=["bn6b2"])
                P.emit("vector", lambda v: v.bn_aggr(mv2[:, 0:2], bn6b2[:]),
                       reads=["bn6a2", "bn6b2"], writes=["mv2"])
                P.emit("vector", lambda v: v.tensor_scalar_add(sc2[:, 0:1], mv2[:, 1:2], EPS),
                       reads=["mv2"], writes=["sc20"])
                emit_rsqrt(sc2[:, 0:1], "sc20", sc2[:, 2:3], "rstd2", sc2[:, 1:2], "sc21")
                P.emit("vector", lambda v: v.tensor_scalar(
                    z_tm[:], retr[:], mv2[:, 0:1], sc2[:, 2:3],
                    op0=ALU.subtract, op1=ALU.mult),
                    reads=["retr", "mv2", "rstd2"], writes=["z_tm"])
                transpose_pe(z_tm[:], "z_tm", z_fm[:, ts_(u, 1024)], "z_fm", pA0, "pA0")
                for h2 in range(2):
                    for c in range(NC8):
                        first = (h2 == 0 and c == 0)
                        P.emit("tensor", lambda t, u=u, h2=h2, c=c: t.matmul(
                            pA1[:, h2 * 512:h2 * 512 + 512],
                            z_fm[:, u * 1024 + c * 128: u * 1024 + c * 128 + 128],
                            W_sb["o"][:, (c * 8 + h2 * 4) * 128:(c * 8 + h2 * 4) * 128 + 512],
                            start=(c == 0), stop=False),
                            reads=["z_fm", "W_o"],
                            writes=["pA1"] if first else [])
                    P.emit("tensor", lambda t, h2=h2: t.matmul(
                        pA1[:, h2 * 512:h2 * 512 + 512], ones1b[:],
                        bo_bf[:, h2 * 512:(h2 + 1) * 512], start=False, stop=True),
                        reads=["ones1b", "bo_bf"],
                        writes=["pA1"] if h2 == 1 else [])
                P.emit("vector", lambda v, u=u: v.scalar_tensor_tensor(
                    t_x[:], pA1[:, 0:1024], 1.0, xa[:, ts_(u, 1024)],
                    op0=ALU.mult, op1=ALU.add),
                    reads=["pA1", "xa"], writes=["t_x"])
                P.emit("gpsimd", lambda s, i=i, u=u: s.dma_start(
                    xmid_h[i * TA + u * 128: i * TA + (u + 1) * 128, :], t_x[:]),
                    reads=["t_x"], writes=["xmid_done"], dma=True)

        if debug_stop == "wdump":
            wdump_h = nc.dram_tensor("wdump", [6, 128, 8192], BF16, kind="ExternalOutput").ap()
            for k, n in enumerate(("k", "v", "q", "kp", "qp", "o")):
                P.emit("gpsimd", lambda g, k=k, n=n: g.dma_start(wdump_h[k], W_sb[n][:]),
                       reads=["W_" + n, "xmid_done"], writes=["wdump_done"], dma=True)
        if debug_stop is not None:
            P.emit("gpsimd", lambda g: g.engine_nop(), reads=["xmid_done", "wdump_done"] if debug_stop == "wdump" else ["xmid_done"])
            a_stack.close()
            P.build()
            return nc

        snap = dict(P.cnt)
        for e in P.ENGINES:
            waits = tuple((o, snap[o]) for o in P.ENGINES if o != e and snap[o] > 0)
            P.cnt[e] += 1
            P.ops[e].append((waits, lambda eng: eng.nop(), 1))
        a_stack.close()

        f_stack = ExitStack()

        def sbf(name, shape, dtype):
            return f_stack.enter_context(nc.sbuf_tensor(name, shape, dtype))

        f1_sb = sbf("f1_sb", [128, 8 * ES * FF], BF16)
        D_sb = sbf("D_sb", [128, 8 * ES * 32], BF16)
        Ir = sbf("Ir", [128, 32 * 512], BF16)
        hi_tm = sbf("hi_tm", [128, 8192], BF16)
        lo_ts = sbf("lo_ts", [128, D], BF16)
        t1f = sbf("t1f", [128, D], F32)
        xbf = [sbf(f"xbf{i}", [128, D], F32) for i in range(2)]
        tie_sb = sbf("tie_sb", [128, TF], BF16)
        mb = sbf("mb", [128, 32], F32)
        perm_fm = sbf("perm_fm", [128, TF], BF16)
        perm_tm = sbf("perm_tm", [128, 8 * ES], BF16)
        chi = sbf("chi", [128, 8 * 128], BF16)
        clo = sbf("clo", [128, 8 * 128], BF16)
        biasq2 = sbf("biasq2_sb", [32, 128], F32)
        fbn6 = sbf("fbn6", [128, 12], F32)
        fmv = sbf("fmv", [128, 2], F32)
        fsc = sbf("fsc", [128, 4], F32)
        fident = sbf("fident", [128, 128], BF16)
        pexp = sbf("pexp", [128, 512], BF16)
        hi_fm = D_sb[:, 0:8192]
        lo_fm = D_sb[:, 8192:16384]
        logit_sb = D_sb[:, 16384:18432].bitcast(F32)
        innerf = D_sb[0:32, 16384:24576].bitcast(F32)
        f2blk = [D_sb[0:32, 8192:16384], D_sb[0:32, 16384:24576]]
        stgbig = D_sb[0:32, 24576:32768]
        f2st = D_sb[0:32, 4096:12288].bitcast(F32)
        inner = D_sb[0:32, 0:4096]
        sctf = t1f[0:32, 0:0 + 1024]

        P.emit("vector", lambda v: v.memset(pexp[:], 0.0), writes=["pexp"])
        P.emit("gpsimd", lambda g: g.dma_start(xbf[0][:, 0:128], ident_h),
               writes=["xbf0"], dma=True)
        P.emit("vector", lambda v: v.tensor_copy(fident[:], xbf[0][:, 0:128]),
               reads=["xbf0"], writes=["fident"])
        for c in range(NC8):
            for j in range(4):
                sl = slice(j * 1024, (j + 1) * 1024)
                P.emit("gpsimd", lambda g, c=c, sl=sl: g.dma_start(
                    xbf[1][:], f1_h[c, :, sl]), writes=["xbf1"], dma=True)
                P.emit("vector", lambda v, c=c, sl=sl: v.tensor_copy(
                    f1_sb[:, c * 4096 + sl.start: c * 4096 + sl.stop], xbf[1][:]),
                    reads=["xbf1"], writes=["f1_sb"])
        for c in range(NC8):
            P.emit("gpsimd", lambda g, c=c: g.dma_start(
                xbf[0][:, 0:128], ctrl_h[ts_(c, 128), :]), writes=["xbf0"], dma=True)
            P.emit("vector", lambda v, c=c: v.tensor_copy(
                chi[:, ts_(c, 128)], xbf[0][:, 0:128]), reads=["xbf0"], writes=["chi"])
            P.emit("vector", lambda v, c=c: v.tensor_sub(
                clo[:, ts_(c, 128)], xbf[0][:, 0:128], chi[:, ts_(c, 128)]),
                reads=["xbf0", "chi"], writes=["clo"])
        P.emit("gpsimd", lambda g: g.dma_start(biasq2[:], biasq2_h),
               writes=["biasq"], dma=True)
        P.emit("gpsimd", lambda g: g.dma_start(xbf[0][:], tie_h[:, 0:1024]),
               writes=["xbf0"], dma=True)
        P.emit("vector", lambda v: v.tensor_copy(tie_sb[:], xbf[0][:]),
               reads=["xbf0"], writes=["tie_sb"])
        for j in range(32):
            sl = slice(j * 4096, (j + 1) * 4096)
            P.emit("gpsimd", lambda g, sl=sl: g.dma_start(f2st, f2q2_h[:, sl]),
                   writes=["D"], dma=True)
            P.emit("vector", lambda v: v.tensor_copy(stgbig[:, 0:4096], f2st),
                   reads=["D"], writes=["D"])
            P.emit("gpsimd", lambda g, sl=sl: g.dma_start(f2bf_h[:, sl], stgbig[:, 0:4096]),
                   reads=["D"], writes=["f2bf"], dma=True)

        def transpose_f(src_ap, srcname, dst_ap, dstname, slot_ap, slotname):
            slotb = slot_ap.bitcast(BF16)
            for c in range(NC8):
                P.emit("tensor", lambda t, c=c, slotb=slotb, src_ap=src_ap: t.transpose(
                    slotb[:, c * 128:(c + 1) * 128], src_ap[:, c * 128:(c + 1) * 128],
                    fident[:]),
                    reads=[srcname, "fident"],
                    writes=[slotname] if c in (0, NC8 - 1) else [])
            P.emit("vector", lambda v, slotb=slotb, dst_ap=dst_ap: v.tensor_copy(
                dst_ap, slotb[:, 0:1024]),
                reads=[slotname], writes=[dstname])

        NQ_EMIT = 1 if f_stop else NQF
        for qq in range(NQ_EMIT):
            for w in range(8):
                xb = xbf[w % 2]
                xbn = f"xbf{w % 2}"
                row0 = qq * TF + w * 128
                P.emit("gpsimd", lambda g, row0=row0, xb=xb: g.dma_start(
                    xb[:], xmid_h[row0:row0 + 128, :]),
                    reads=["xmid_done"], writes=[xbn], dma=True)
                P.emit("gpsimd", lambda g, row0=row0, xb=xb: g.dma_start(
                    out_h[row0:row0 + 128, :], xb[:]),
                    reads=[xbn], writes=["out_pre"], dma=True)
                P.emit("vector", lambda v, xb=xb: v.bn_stats(fbn6[:, 0:6], xb[:, 0:512]),
                       reads=[xbn], writes=["fbn6a"])
                P.emit("vector", lambda v, xb=xb: v.bn_stats(fbn6[:, 6:12], xb[:, 512:1024]),
                       reads=[xbn], writes=["fbn6b"])
                P.emit("vector", lambda v: v.bn_aggr(fmv[:], fbn6[:]),
                       reads=["fbn6a", "fbn6b"], writes=["fmv"])
                P.emit("vector", lambda v: v.tensor_scalar_add(fsc[:, 0:1], fmv[:, 1:2], EPS),
                       reads=["fmv"], writes=["fsc0"])
                emit_rsqrt(fsc[:, 0:1], "fsc0", fsc[:, 2:3], "frstd", fsc[:, 1:2], "fsc1")
                P.emit("vector", lambda v, xb=xb: v.tensor_scalar(
                    t1f[:], xb[:], fmv[:, 0:1], fsc[:, 2:3],
                    op0=ALU.subtract, op1=ALU.mult),
                    reads=[xbn, "fmv", "frstd"], writes=["t1f"])
                P.emit("vector", lambda v, w=w: v.tensor_copy(
                    hi_tm[:, ts_(w, 1024)], t1f[:]),
                    reads=["t1f"], writes=["hi_tm"])
                P.emit("vector", lambda v, w=w: v.tensor_sub(
                    lo_ts[:], t1f[:], hi_tm[:, ts_(w, 1024)]),
                    reads=["t1f", "hi_tm"], writes=["lo_ts"])
                transpose_f(hi_tm[:, ts_(w, 1024)], "hi_tm",
                            hi_fm[:, ts_(w, 1024)], "D", pA0, "pA0")
                transpose_f(lo_ts[:], "lo_ts",
                            lo_fm[:, ts_(w, 1024)], "D", pA0, "pA0")

            if f_stop == "ln3":
                break
            terms = ((chi, "chi", hi_fm), (clo, "clo", hi_fm), (chi, "chi", lo_fm))
            for wh in range(2):
                k = 0
                for (cw, cwn, act) in terms:
                    for c in range(NC8):
                        first = (k == 0)
                        last = (k == 3 * NC8 - 1)
                        act3 = act.rearrange("p (w q) -> p w q", q=1024)
                        P.emit("tensor", lambda t, wh=wh, c=c, cw=cw, act3=act3, first=first, last=last: t.matmul(
                            psB[:, wh * 512:(wh + 1) * 512].rearrange("p (w q) -> p w q", q=128),
                            cw[:, ts_(c, 128)],
                            act3[:, 4 * wh:4 * wh + 4, c * 128:(c + 1) * 128],
                            start=first, stop=last),
                            reads=[cwn, "D"],
                            writes=["pB"] if ((first and wh == 0) or (last and wh == 1)) else [])
                        k += 1
            P.emit("vector", lambda v: v.tensor_add(logit_sb, psB[:, 0:1024], tie_sb[:]),
                   reads=["pB", "tie_sb"], writes=["D"])
            P.emit("vector", lambda v: v.reduce_max(
                mb[:], logit_sb.rearrange("p (a b) -> p a b", b=32), axis=AX.X),
                reads=["D"], writes=["mb"])
            P.emit("vector", lambda v: v.tensor_tensor(
                perm_fm[:].rearrange("p (a b) -> p a b", b=32),
                logit_sb.rearrange("p (a b) -> p a b", b=32),
                mb[:].unsqueeze(2).broadcast_to([128, 32, 32]), op=ALU.is_equal),
                reads=["D", "mb"], writes=["perm_fm"])
            slotb = pA0.bitcast(BF16)
            for w in range(8):
                P.emit("tensor", lambda t, w=w, slotb=slotb: t.transpose(
                    slotb[:, w * 128:(w + 1) * 128], perm_fm[:, w * 128:(w + 1) * 128],
                    fident[:]),
                    reads=["perm_fm", "fident"],
                    writes=["pA0"] if w in (0, 7) else [])
            P.emit("vector", lambda v, slotb=slotb: v.tensor_copy(perm_tm[:], slotb[:, 0:1024]),
                   reads=["pA0"], writes=["perm_tm"])

            if f_stop == "perm":
                break
            for w in range(8):
                for gi in range(4):
                    P.emit("vector", lambda v, w=w, gi=gi: v.tensor_copy(
                        pexp[32 * gi:32 * gi + 32, gi * 128:(gi + 1) * 128],
                        perm_tm[32 * gi:32 * gi + 32, w * 128:(w + 1) * 128]),
                        reads=["perm_tm"], writes=["pexp"])
                for c in range(NC8):
                    slot = pA1 if ((w * 8 + c) % 2) else pA0
                    slotn = "pA1" if ((w * 8 + c) % 2) else "pA0"
                    P.emit("tensor", lambda t, w=w, c=c, slot=slot: t.matmul(
                        slot[:, 0:512],
                        hi_tm[:, w * 1024 + c * 128: w * 1024 + (c + 1) * 128],
                        pexp[:],
                        start=True, stop=True),
                        reads=["hi_tm", "pexp"],
                        writes=[slotn])
                    src = slot[:, 0:512].rearrange("p (g e) -> p e g", g=4)
                    dst = D_sb[:, c * 4096:(c + 1) * 4096].rearrange(
                        "p (e g) -> p e g", g=32)[:, :, 4 * w:4 * w + 4]
                    P.emit("scalar", lambda sc_, src=src, dst=dst: sc_.copy(dst, src),
                           reads=[slotn], writes=["D"])

            if f_stop == "dispatch":
                break
            def f1_region(e):
                if e < 32:
                    return pA0[0:32, e * 32:(e + 1) * 32], "pA0"
                if e < 64:
                    return pA1[0:32, (e - 64 + 32) * 32:(e - 32 + 1) * 32] if False else pA1[0:32, (e - 32) * 32:(e - 31) * 32], "pA1"
                return psB[0:32, (e - 64) * 32:(e - 63) * 32], "pB"
            for e in range(ES):
                reg, regn = f1_region(e)
                for c in range(NC8):
                    first = (c == 0 and e in (0, 32, 64))
                    last = (c == NC8 - 1 and e in (31, 63, 127))
                    P.emit("tensor", lambda t, e=e, c=c, reg=reg: t.matmul(
                        reg,
                        f1_sb[:, (c * ES + e) * FF:(c * ES + e) * FF + FF],
                        D_sb[:, c * 4096 + e * 32: c * 4096 + e * 32 + 32],
                        start=(c == 0), stop=(c == NC8 - 1)),
                        reads=["f1_sb", "D"],
                        writes=[regn] if (first or last) else [])
            for (reg, regn, e0, ne) in ((pA0[0:32, :], "pA0", 0, 32), (pA1[0:32, :], "pA1", 32, 32), (psB[0:32, 0:2048], "pB", 64, 64)):
                bb = biasq2[:, e0:e0 + ne].unsqueeze(2).broadcast_to([32, ne, 32])
                P.emit("vector", lambda v, reg=reg, e0=e0, ne=ne, bb=bb: v.tensor_tensor(
                    innerf[:, 0:ne * 32].rearrange("p (e g) -> p e g", g=32),
                    reg.rearrange("p (e g) -> p e g", g=32), bb, op=ALU.add),
                    reads=[regn, "biasq"], writes=["D"])
                P.emit("vector", lambda v, e0=e0, ne=ne: v.tensor_scalar_max(
                    inner[:, e0 * 32:(e0 + ne) * 32], innerf[:, 0:ne * 32], 0.0),
                    reads=["D"], writes=["D"])

            if f_stop == "f1":
                break
            for pb_ in (32, 64, 96):
                P.emit("vector", lambda v, pb_=pb_: v.memset(D_sb[pb_:pb_ + 32, 0:4096], 0.0),
                       reads=[], writes=["Din", "D"])
                P.emit("vector", lambda v, pb_=pb_: v.memset(D_sb[pb_:pb_ + 32, 8192:24576], 0.0),
                       reads=[], writes=["Df2a", "Df2b", "D"])
            for hd in range(2):
                for eb in range(8):
                    fb = f2blk[eb % 2]
                    fbF = D_sb[:, 8192 + (eb % 2) * 8192: 8192 + (eb % 2) * 8192 + 8192]
                    fbn_ = "Df2a" if (eb % 2 == 0) else "Df2b"
                    innerF = D_sb[:, 0:4096]
                    P.emit("gpsimd", lambda g, eb=eb, hd=hd, fbF=fbF: g.dma_start(
                        fbF[0:32, :].rearrange("p (e d) -> p e d", d=512),
                        f2bf_h[:, eb * 16384:(eb + 1) * 16384].rearrange(
                            "p (e d) -> p e d", d=1024)[:, :, hd * 512:hd * 512 + 512]),
                        reads=["f2bf"], writes=[fbn_, "D"], dma=True)
                    for ei in range(16):
                        e = eb * 16 + ei
                        slot = pA0 if (e % 2 == 0) else pA1
                        slotn = "pA0" if (e % 2 == 0) else "pA1"
                        P.emit("tensor", lambda t, e=e, ei=ei, fbF=fbF, slot=slot: t.matmul(
                            slot[0:32, 0:512],
                            innerF[:, e * 32:(e + 1) * 32],
                            fbF[:, ei * 512:(ei + 1) * 512],
                            start=True, stop=True),
                            reads=["Din", fbn_],
                            writes=[slotn])
                        P.emit("scalar", lambda sc_, ei=ei, slot=slot: sc_.copy(
                            stgbig[:, ei * 512:(ei + 1) * 512], slot[0:32, 0:512]),
                            reads=[slotn], writes=["Dstg"])
                    P.emit("gpsimd", lambda g: g.dma_start(rl_h, stgbig[:, 0:8192]),
                           reads=["Dstg"], writes=["rl"], dma=True)
                    src = rl_h.rearrange("g (e d) -> e g d", d=512)
                    dst = Ir[eb * 16:(eb + 1) * 16, :].rearrange("p (g d) -> p g d", d=512)
                    P.emit("gpsimd", lambda g, src=src, dst=dst: g.dma_start(dst, src),
                           reads=["rl"], writes=["Ir"], dma=True)
                if f_stop == "f2":
                    continue
                for w in range(8):
                    for gi in range(4):
                        gg = 4 * w + gi
                        P.emit("tensor", lambda t, w=w, gi=gi, gg=gg: t.matmul(
                            psB[0:32, gi * 512:(gi + 1) * 512],
                            perm_fm[:, w * 128 + 32 * gi: w * 128 + 32 * gi + 32],
                            Ir[:, gg * 512:(gg + 1) * 512],
                            start=True, stop=True),
                            reads=["perm_fm", "Ir"],
                            writes=["pB"] if gi in (0, 3) else [])
                    P.emit("vector", lambda v: v.tensor_copy(t1f[0:32, 0:1024], psB[0:32, 0:1024]),
                           reads=["pB"], writes=["t1f"])
                    P.emit("vector", lambda v: v.tensor_copy(t1f[0:32, 0:0 + 1024], t1f[0:32, 0:1024]) if False else v.tensor_copy(lo_ts[0:32, 0:1024].bitcast(F32)[:, 0:512], psB[0:32, 1024:1536]),
                           reads=["pB"], writes=["lo_ts"])
                    P.emit("vector", lambda v: v.tensor_copy(lo_ts[0:32, 0:1024].bitcast(F32)[:, 512:1024] if False else xbf[0][0:32, 512:1024], psB[0:32, 1536:2048]),
                           reads=["pB"], writes=["xbf0"])
                    row0 = qq * TF + w * 128
                    P.emit("gpsimd", lambda g, row0=row0, hd=hd: g.dma_start(
                        out_h[row0:row0 + 64, hd * 512:hd * 512 + 512].rearrange(
                            "(g p) d -> p g d", p=32),
                        t1f[0:32, 0:1024].rearrange("p (g d) -> p g d", d=512),
                        accum_op=ALU.add),
                        reads=["t1f", "out_pre"], writes=["out_done"], dma=True)
                    P.emit("gpsimd", lambda g, row0=row0, hd=hd: g.dma_start(
                        out_h[row0 + 64:row0 + 96, hd * 512:hd * 512 + 512],
                        lo_ts[0:32, 0:1024].bitcast(F32)[:, 0:512],
                        accum_op=ALU.add),
                        reads=["lo_ts", "out_pre"], writes=["out_done"], dma=True)
                    P.emit("gpsimd", lambda g, row0=row0, hd=hd: g.dma_start(
                        out_h[row0 + 96:row0 + 128, hd * 512:hd * 512 + 512],
                        xbf[0][0:32, 512:1024],
                        accum_op=ALU.add),
                        reads=["xbf0", "out_pre"], writes=["out_done"], dma=True)

        if f_stop is None:
            P.emit("gpsimd", lambda g: g.nop(), reads=["out_done"])
        else:
            allb = [b for b in ("out_done", "Ir", "D", "perm_tm", "hi_tm") if b in P.WR]
            for b in allb:
                P.emit("gpsimd", lambda g: g.nop(), reads=[b])
        f_stack.close()
        P.build()
        return nc


def _host_inputs(inp):
    x = np.ascontiguousarray(np.asarray(inp["L0_attn_x"], dtype=np.float32).reshape(16 * 2048, 1024))
    def v8(a):
        return np.ascontiguousarray(np.asarray(a, dtype=np.float32).reshape(8, 128).T)
    base = {
        "Wk": np.ascontiguousarray(inp["L0_attn_Wk"], dtype=np.float32),
        "Wv": np.ascontiguousarray(inp["L0_attn_Wv"], dtype=np.float32),
        "Wq": np.ascontiguousarray(inp["L0_attn_Wq"], dtype=np.float32),
        "Wkp": np.ascontiguousarray(inp["L0_attn_Wkp"], dtype=np.float32),
        "Wqp": np.ascontiguousarray(inp["L0_attn_Wqp"], dtype=np.float32),
        "Wo": np.ascontiguousarray(inp["L0_attn_Wo"], dtype=np.float32),
        "bk": v8(inp["L0_attn_bk"]), "bv": v8(inp["L0_attn_bv"]), "bq": v8(inp["L0_attn_bq"]),
        "bkp": v8(inp["L0_attn_bkp"]), "bqp": v8(inp["L0_attn_bqp"]),
        "ps": v8(inp["L0_attn_phase_scale"]),
        "lng": v8(inp["L0_attn_ln_g"]), "lnb": v8(inp["L0_attn_ln_b"]),
        "bo": np.ascontiguousarray(np.asarray(inp["L0_attn_bo"], dtype=np.float32).reshape(1, 1024)),
    }
    k = np.arange(128)
    u64 = ((k[:, None] // 64 == k[None, :] // 64) & (k[:, None] <= k[None, :]))
    base["u64"] = u64.astype(np.float32)
    base["ident"] = np.eye(128, dtype=np.float32)
    base["bvr"] = np.ascontiguousarray(np.broadcast_to(
        np.asarray(inp["L0_attn_bv"], dtype=np.float32)[None, :], (128, 1024)))
    base["bkpr"] = np.ascontiguousarray(np.broadcast_to(
        np.asarray(inp["L0_attn_bkp"], dtype=np.float32)[None, :], (128, 1024)))
    base["bqpr"] = np.ascontiguousarray(np.broadcast_to(
        np.asarray(inp["L0_attn_bqp"], dtype=np.float32)[None, :], (128, 1024)))
    base["psr"] = np.ascontiguousarray(np.broadcast_to(
        np.asarray(inp["L0_attn_phase_scale"], dtype=np.float32)[None, :], (128, 1024)))
    tie = np.linspace(0.0, 1e-6, 32, dtype=np.float32)
    base["tie"] = np.ascontiguousarray(np.broadcast_to(np.tile(tie, TF // 32)[None, :], (128, TF)))
    ctrl = np.ascontiguousarray(np.asarray(inp["L0_ffn_controller"], dtype=np.float32).reshape(1024, 128))
    f1 = np.ascontiguousarray(np.asarray(inp["L0_ffn_f1"], dtype=np.float32).reshape(8, 128, 128 * 32))
    fb = np.asarray(inp["L0_ffn_bias"], dtype=np.float32).reshape(128, 32)
    f2 = np.asarray(inp["L0_ffn_f2"], dtype=np.float32).reshape(128, 32, 1024)
    e = np.arange(128)
    r_, q_ = e // 32, e % 32
    biasq = np.zeros((128, 32), dtype=np.float32)
    f2q = np.zeros((128, 32 * 1024), dtype=np.float32)
    for ei in range(128):
        rr, qq = ei // 32, ei % 32
        biasq[32 * rr: 32 * rr + 32, qq] = fb[ei]
        f2q[32 * rr: 32 * rr + 32, qq * 1024:(qq + 1) * 1024] = f2[ei]
    base["ctrl"] = ctrl
    base["f1"] = f1
    base["biasq"] = np.ascontiguousarray(biasq)
    base["f2q"] = np.ascontiguousarray(f2q)
    base["biasq2"] = np.ascontiguousarray(fb.T)
    base["f2q2"] = np.ascontiguousarray(np.transpose(f2, (1, 0, 2)).reshape(32, 128 * 1024))

    in_maps = []
    for i in range(8):
        m = dict(base)
        m["x"] = np.ascontiguousarray(x[i * T:(i + 1) * T])
        in_maps.append(m)
    return in_maps


_DEBUG_STOP = None


def kernel(**inputs):
    in_maps = _host_inputs(inputs)
    nc = build_nc(debug_stop=_DEBUG_STOP)
    res = run_bass_kernel_spmd(nc, in_maps, core_ids=list(range(NCORES)))
    key = "xmid" if _DEBUG_STOP == "attn" else "out"
    out = np.concatenate([np.asarray(r[key]) for r in res.results], axis=0)
    return out.reshape(16, 2048, 1024)


# version 21
# speedup vs baseline: 1.0698x; 1.0233x over previous
import numpy as np
import ml_dtypes
from contextlib import ExitStack
import concourse.bass as bass
import concourse.mybir as mybir
from concourse.bass_utils import run_bass_kernel_spmd

F32 = mybir.dt.float32
BF16 = mybir.dt.bfloat16
AF = mybir.ActivationFunctionType
ALU = mybir.AluOpType
AX = mybir.AxisListType

NCORES = 8
T = 4096
D = 1024
NC8 = 8
TA = 256
NTA = T // TA
TF = 1024
NQF = T // TF
ES = 128
FF = 32
EPS = 1e-5
PI = float(np.pi)


class Prog:
    ENGINES = ("sync", "tensor", "vector", "scalar", "gpsimd")

    def __init__(self, nc, stack):
        self.nc = nc
        self.ops = {e: [] for e in self.ENGINES}
        self.sems = {}
        self.cnt = {}
        for e in self.ENGINES:
            self.sems[e] = stack.enter_context(nc.semaphore("sem_" + e))
            self.cnt[e] = 0
        self.WR = {}
        self.LR = {}
        self._dma_cnt = {}
        self._last_tp = None

    DMA_ENGINES = ("sync", "gpsimd", "scalar")

    def emit(self, eng, fn, reads=(), writes=(), dma=False, transpose=False):
        waits = []
        for b in reads:
            waits += self.WR.get(b, [])
        for b in writes:
            waits += self.LR.pop(b, [])
            waits += self.WR.get(b, [])
        amt = 16 if dma else 1
        self.cnt[eng] += amt
        ev = (eng, self.cnt[eng])
        wmax = {}
        for (s, v) in waits:
            if wmax.get(s, -1) < v:
                wmax[s] = v
        self.ops[eng].append((tuple(wmax.items()), fn, amt))
        for b in reads:
            self.LR.setdefault(b, []).append(ev)
        for b in writes:
            self.WR[b] = [ev]
        return ev

    def barrier_event(self, bufs):
        evs = []
        for b in bufs:
            evs += self.WR.get(b, []) + self.LR.get(b, [])
        return evs

    def build(self):
        nc = self.nc
        with nc.Block() as block:
            for e in self.ENGINES:
                ops = self.ops[e]
                sems = self.sems
                mysem = sems[e]

                def body(engine, ops=ops, e=e):
                    last = {}
                    for waits, fn, amt in ops:
                        for (s, v) in waits:
                            if last.get(s, 0) >= v:
                                continue
                            engine.wait_ge(sems[s], v)
                            last[s] = v
                        inst = fn(engine)
                        inst.then_inc(mysem, amt)
                getattr(block, e)(body)


def ts_(i, n):
    return slice(i * n, (i + 1) * n)


def build_nc(debug_stop=None, f_stop=None):
    nc = bass.Bass("TRN2", target_bir_lowering=False)
    dt_in = {}

    def din(name, shape, dtype=F32):
        t = nc.dram_tensor(name, shape, dtype, kind="ExternalInput").ap()
        dt_in[name] = t
        return t

    x_h = din("x", [T, D])
    Wk_h = din("Wk", [D, D]); Wv_h = din("Wv", [D, D]); Wq_h = din("Wq", [D, D])
    Wkp_h = din("Wkp", [D, D]); Wqp_h = din("Wqp", [D, D]); Wo_h = din("Wo", [D, D])
    bk_h = din("bk", [128, 8]); bv_h = din("bv", [128, 8]); bq_h = din("bq", [128, 8])
    bkp_h = din("bkp", [128, 8]); bqp_h = din("bqp", [128, 8])
    ps_h = din("ps", [128, 8]); lng_h = din("lng", [128, 8]); lnb_h = din("lnb", [128, 8])
    bo_h = din("bo", [1, D])
    u64_h = din("u64", [128, 128])
    ident_h = din("ident", [128, 128])
    bvr_h = din("bvr", [128, D]); bkpr_h = din("bkpr", [128, D]); bqpr_h = din("bqpr", [128, D])
    psr_h = din("psr", [128, D])
    ctrl_h = din("ctrl", [D, ES])
    f1_h = din("f1", [NC8, 128, ES * FF])
    f2q_h = din("f2q", [128, 32 * D])
    biasq_h = din("biasq", [128, 32])
    f2q2_h = din("f2q2", [32, 128 * D])
    biasq2_h = din("biasq2", [32, 128])
    tie_h = din("tie", [128, TF])

    out_h = nc.dram_tensor("out", [T, D], F32, kind="ExternalOutput").ap()
    if debug_stop is not None:
        xmid_h = nc.dram_tensor("xmid", [T, D], F32, kind="ExternalOutput").ap()
        dbg_h = nc.dram_tensor("dbg", [T, D], F32, kind="ExternalOutput").ap()
    else:
        xmid_h = nc.dram_tensor("xmid", [T, D], F32, kind="Internal").ap()
        dbg_h = None
    f2bf_h = nc.dram_tensor("f2bf", [32, 128 * D], BF16, kind="Internal").ap()
    rl_h = nc.dram_tensor("rl", [32, 8192], BF16, kind="Internal").ap()
    rl2_h = nc.dram_tensor("rl2", [32, 8192], BF16, kind="Internal").ap()

    with ExitStack() as stack:
        P = Prog(nc, stack)

        def sb(name, shape, dtype):
            return stack.enter_context(nc.sbuf_tensor(name, shape, dtype))


        a_stack = ExitStack()

        def sba(name, shape, dtype):
            return a_stack.enter_context(nc.sbuf_tensor(name, shape, dtype))

        W_sb = {n: sba("W_" + n, [128, 8192], BF16) for n in ("k", "v", "q", "kp", "qp", "o")}
        vec_sb = {n: sba("vec_" + n, [128, 8], F32)
                  for n in ("bk", "bv", "bq", "bkp", "bqp", "ps", "aps", "lng", "lnb")}
        u64_sb = sba("u64_sb", [128, 128], BF16)
        cpihalf = sba("cpihalf", [128, 1], F32)
        wtmp = [sba(f"wtmp{i}", [128, 1024], F32) for i in range(2)]
        wtmpb = sba("wtmpb", [128, 1024], BF16)
        bo_row = wtmp[0][0:1, 0:1024]

        xbuf = [sba(f"x{i}", [128, TA * 8], F32) for i in range(2)]
        xa = sba("xa", [128, TA * 8], F32)
        xn_tm = sba("xn_tm", [128, TA * 8], BF16)
        xn_fm = sba("xn_fm", [128, TA * 8], BF16)
        Kf = sba("Kf", [128, TA * 8], BF16)
        Qf = sba("Qf", [128, TA * 8], BF16)
        Vf = sba("Vf", [128, TA * 8], BF16)
        t0 = sba("t0", [128, TA * 8], F32)
        th = sba("th", [128, TA * 8], F32)
        sinb = sba("sinb", [128, TA * 8], BF16)
        cosb = sba("cosb", [128, TA * 8], BF16)
        br_tm = sba("br_tm", [128, TA * 8], BF16)
        bi_tm = sba("bi_tm", [128, TA * 8], BF16)
        rep = {n: sba("rep_" + n, [128, D], BF16) for n in ("bv", "bkp", "bqp", "ps")}
        ident_sb = sba("ident_sb", [128, 128], BF16)
        retr = sba("retr", [128, D], BF16)
        z_tm = sba("z_tm", [128, D], BF16)
        z_fm = sba("z_fm", [128, TA * 8], BF16)
        t_x = sba("t_x", [128, D], F32)
        r1 = th[:, 0:1024]
        r2 = t_x[:]
        bn6 = sba("bn6", [128, 12], F32)
        mv = sba("mv", [128, 4], F32)
        sc = sba("sc", [128, 8], F32)
        sc2 = sba("sc2", [128, 4], F32)
        bn6b2 = sba("bn6b2", [128, 12], F32)
        mv2 = sba("mv2", [128, 2], F32)
        psA = stack.enter_context(nc.psum_tensor("psA", [128, 2048], F32))
        psB = stack.enter_context(nc.psum_tensor("psB", [128, 2048], F32))

        I32 = mybir.dt.int32

        def emit_rsqrt(tv_ap, tv_name, out_ap, out_name, scr_ap, scr_name):
            ti = tv_ap.bitcast(I32)
            oi = out_ap.bitcast(I32)
            P.emit("vector", lambda v: v.tensor_single_scalar(oi, ti, 1, op=ALU.logical_shift_right),
                   reads=[tv_name], writes=[out_name])
            P.emit("vector", lambda v: v.tensor_scalar(oi, oi, -1, 0x5F3759DF, op0=ALU.mult, op1=ALU.add),
                   reads=[out_name], writes=[out_name])
            for _ in range(2):
                P.emit("vector", lambda v: v.tensor_mul(scr_ap, out_ap, out_ap),
                       reads=[out_name], writes=[scr_name])
                P.emit("vector", lambda v: v.tensor_scalar(scr_ap, scr_ap, -0.5, None, op0=ALU.mult),
                       reads=[scr_name], writes=[scr_name])
                P.emit("vector", lambda v: v.tensor_mul(scr_ap, scr_ap, tv_ap),
                       reads=[scr_name, tv_name], writes=[scr_name])
                P.emit("vector", lambda v: v.tensor_scalar_add(scr_ap, scr_ap, 1.5),
                       reads=[scr_name], writes=[scr_name])
                P.emit("vector", lambda v: v.tensor_mul(out_ap, out_ap, scr_ap),
                       reads=[scr_name, out_name], writes=[out_name])

        for n, h in (("k", Wk_h), ("v", Wv_h), ("q", Wq_h), ("kp", Wkp_h), ("qp", Wqp_h)):
            for c in range(NC8):
                buf = wtmp[c % 2]
                bname = f"wtmp{c % 2}"
                P.emit("gpsimd", lambda g, buf=buf, h=h, c=c: g.dma_start(buf[:], h[ts_(c, 128), :]),
                       writes=[bname], dma=True)
                P.emit("vector", lambda v, n=n, c=c, buf=buf: v.tensor_copy(
                    W_sb[n][:, ts_(c, 1024)], buf[:]),
                    reads=[bname], writes=["W_" + n])
        for n, h in (("bk", bk_h), ("bv", bv_h), ("bq", bq_h), ("bkp", bkp_h),
                     ("bqp", bqp_h), ("ps", ps_h), ("lng", lng_h), ("lnb", lnb_h)):
            P.emit("gpsimd", lambda s, n=n, h=h: s.dma_start(vec_sb[n][:], h),
                   writes=["vec_" + n], dma=True)
        for (tname, hsrc, dstap, dwr) in (
                ("u64", u64_h, u64_sb[:], "u64"), ("ident", ident_h, ident_sb[:], "ident")):
            P.emit("gpsimd", lambda g, hsrc=hsrc: g.dma_start(wtmp[0][:, 0:128], hsrc),
                   writes=["wtmp0"], dma=True)
            P.emit("vector", lambda v, dstap=dstap: v.tensor_copy(dstap, wtmp[0][:, 0:128]),
                   reads=["wtmp0"], writes=[dwr])
        for n, h in (("bv", bvr_h), ("bkp", bkpr_h), ("bqp", bqpr_h), ("ps", psr_h)):
            P.emit("gpsimd", lambda g, h=h: g.dma_start(wtmp[1][:], h),
                   writes=["wtmp1"], dma=True)
            P.emit("vector", lambda v, n=n: v.tensor_copy(rep[n][:], wtmp[1][:]),
                   reads=["wtmp1"], writes=["rep_" + n])
        P.emit("scalar", lambda s: s.activation(vec_sb["aps"][:], vec_sb["ps"][:], AF.Tanh),
               reads=["vec_ps"], writes=["vec_aps"])
        P.emit("scalar", lambda s: s.activation(vec_sb["aps"][:], vec_sb["ps"][:], AF.Abs),
               reads=["vec_ps"], writes=["vec_aps"])
        P.emit("vector", lambda v: v.memset(cpihalf[:], PI / 2), writes=["cpihalf"])
        for c in range(NC8):
            buf = wtmp[c % 2]
            bname = f"wtmp{c % 2}"
            src = Wo_h[ts_(c, 128), :]
            P.emit("gpsimd", lambda s, buf=buf, src=src: s.dma_start(buf[:], src),
                   reads=[], writes=[bname], dma=True)
            P.emit("vector", lambda v, c=c, buf=buf: v.tensor_scalar_mul(
                W_sb["o"][:, ts_(c, 1024)], buf[:], vec_sb["lng"][:, c:c + 1]),
                reads=[bname, "vec_lng"], writes=["W_o"])
            for h2 in range(2):
                first = (c == 0 and h2 == 0)
                last = (c == NC8 - 1 and h2 == 1)
                P.emit("tensor", lambda t, c=c, h2=h2, buf=buf: t.matmul(
                    psA[0:1, h2 * 512:(h2 + 1) * 512], vec_sb["lnb"][:, c:c + 1],
                    buf[:, h2 * 512:(h2 + 1) * 512], start=(c == 0), stop=(c == NC8 - 1)),
                    reads=[bname, "vec_lnb"], writes=["psA"] if (first or last) else [])
        P.emit("gpsimd", lambda s: s.dma_start(bo_row, bo_h), writes=["wtmp0"], dma=True)
        P.emit("vector", lambda v: v.tensor_add(bo_row, bo_row, psA[0:1, 0:1024]),
               reads=["psA", "wtmp0"], writes=["wtmp0"])
        ones1b = sba("ones1b", [1, 128], BF16)
        bo_bf = sba("bo_bf", [1, D], BF16)
        P.emit("vector", lambda v: v.memset(ones1b[:], 1.0), writes=["ones1b"])
        P.emit("vector", lambda v: v.tensor_copy(bo_bf[:], bo_row),
               reads=["wtmp0"], writes=["bo_bf"])


        pA0 = psA[:, 0:1024]
        pA1 = psA[:, 1024:2048]

        def proj_fm(wname, in_fm, pslots, inbuf):
            in3 = in_fm.rearrange("p (u w) -> p u w", u=2)
            for j in range(8):
                for c in range(NC8):
                    first = (c == 0 and j == 0)
                    last = (c == NC8 - 1 and j == 7)
                    P.emit("tensor", lambda t, j=j, c=c, in3=in3: t.matmul(
                        psA[:, j * TA:(j + 1) * TA].rearrange("p (u w) -> p u w", u=2),
                        W_sb[wname][:, (c * 8 + j) * 128:(c * 8 + j) * 128 + 128],
                        in3[:, :, c * 128:(c + 1) * 128],
                        start=(c == 0), stop=(c == NC8 - 1)),
                        reads=["W_" + wname, inbuf],
                        writes=["pA0", "pA1"] if (first or last) else [])

        def evac_fm(biasvec, outbuf, outname):
            for j in range(8):
                P.emit("scalar", lambda sc_, j=j: sc_.add(
                    outbuf.rearrange("p (u w) -> p u w", u=2)[:, :, j * 128:(j + 1) * 128],
                    psA[:, j * TA:(j + 1) * TA].rearrange("p (u w) -> p u w", u=2),
                    vec_sb[biasvec][:, j:j + 1]),
                    reads=["pA0", "pA1", "vec_" + biasvec], writes=[outname])

        def proj_tm(wname, lhs_fm, lhsname):
            for u in range(2):
                for h in range(2):
                    for c in range(NC8):
                        first = (u == 0 and h == 0 and c == 0)
                        last = (u == 1 and h == 1 and c == NC8 - 1)
                        P.emit("tensor", lambda t, u=u, h=h, c=c: t.matmul(
                            psB[:, u * 1024 + h * 512: u * 1024 + h * 512 + 512],
                            lhs_fm[:, u * 1024 + c * 128: u * 1024 + c * 128 + 128],
                            W_sb[wname][:, (c * 8 + h * 4) * 128:(c * 8 + h * 4) * 128 + 512],
                            start=(c == 0), stop=(c == NC8 - 1)),
                            reads=["W_" + wname, lhsname],
                            writes=["pB"] if (first or last) else [])

        def evac_tm_rep(repname, outbuf, outname, dtype_note=None):
            for u in range(2):
                P.emit("vector", lambda v, u=u: v.tensor_add(
                    outbuf[:, ts_(u, 1024)], psB[:, ts_(u, 1024)], rep[repname][:]),
                    reads=["pB", "rep_" + repname], writes=[outname])

        def trig_tm():
            P.emit("scalar", lambda s: s.activation(th[:], t0[:], AF.Tanh),
                   reads=["t0"], writes=["th"])
            P.emit("vector", lambda v: v.tensor_mul(
                t0[:].rearrange("p (u w) -> p u w", u=2), th[:].rearrange("p (u w) -> p u w", u=2),
                rep["ps"][:].unsqueeze(1).broadcast_to([128, 2, 1024])),
                reads=["th", "rep_ps"], writes=["t0"])
            P.emit("scalar", lambda s: s.activation(sinb[:], t0[:], AF.Sin),
                   reads=["t0"], writes=["sinb"])
            P.emit("scalar", lambda s: s.activation(th[:], t0[:], AF.Abs),
                   reads=["t0"], writes=["th"])
            P.emit("scalar", lambda s: s.activation(cosb[:], th[:], AF.Sin,
                   bias=cpihalf[:], scale=-1.0),
                   reads=["th", "cpihalf"], writes=["cosb"])

        def transpose_pe(src_ap, srcname, dst_ap, dstname, slot_ap, slotname):
            slotb = slot_ap.bitcast(BF16)
            for c in range(NC8):
                P.emit("tensor", lambda t, c=c, slotb=slotb: t.transpose(
                    slotb[:, c * 128:(c + 1) * 128], src_ap[:, c * 128:(c + 1) * 128],
                    ident_sb[:]),
                    reads=[srcname, "ident"],
                    writes=[slotname] if c in (0, NC8 - 1) else [])
            P.emit("vector", lambda v, slotb=slotb: v.tensor_copy(dst_ap, slotb[:, 0:1024]),
                   reads=[slotname], writes=[dstname])

        for i in range(NTA):
            xb = xbuf[i % 2]
            xname = f"x{i % 2}"
            P.emit("gpsimd", lambda s, i=i, xb=xb: s.dma_start(
                xb[:].rearrange("p (a q) -> p a q", q=1024),
                x_h[ts_(i, TA), :].rearrange("(a p) q -> p a q", p=128)),
                writes=[xname], dma=True)

            for u in range(2):
                P.emit("vector", lambda v, u=u, xb=xb: v.bn_stats(
                    bn6[:, 0:6], xb[:, u * 1024:u * 1024 + 512]),
                    reads=[xname], writes=["bn6a"])
                P.emit("vector", lambda v, u=u, xb=xb: v.bn_stats(
                    bn6[:, 6:12], xb[:, u * 1024 + 512:u * 1024 + 1024]),
                    reads=[xname], writes=["bn6b"])
                P.emit("vector", lambda v, u=u: v.bn_aggr(mv[:, 2 * u:2 * u + 2], bn6[:]),
                       reads=["bn6a", "bn6b"], writes=["mv"])
                P.emit("vector", lambda v, u=u: v.tensor_scalar_add(
                    sc[:, 0:1], mv[:, 2 * u + 1:2 * u + 2], EPS), reads=["mv"], writes=["sc0"])
                emit_rsqrt(sc[:, 0:1], "sc0", sc[:, 2 + u:3 + u], f"rstd{u}", sc[:, 1:2], "sc1")
                P.emit("vector", lambda v, u=u: v.tensor_scalar(
                    sc[:, 4 + u:5 + u], mv[:, 2 * u:2 * u + 1], sc[:, 2 + u:3 + u], -1.0,
                    op0=ALU.mult, op1=ALU.mult), reads=["mv", f"rstd{u}"], writes=[f"nmr{u}"])
                P.emit("vector", lambda v, u=u: v.tensor_scalar_add(
                    sc[:, 6 + u:7 + u], sc[:, 2 + u:3 + u], 1.0),
                    reads=[f"rstd{u}"], writes=[f"s1_{u}"])
                P.emit("vector", lambda v, u=u, xb=xb: v.tensor_scalar(
                    xn_tm[:, ts_(u, 1024)], xb[:, ts_(u, 1024)],
                    mv[:, 2 * u:2 * u + 1], sc[:, 2 + u:3 + u],
                    op0=ALU.subtract, op1=ALU.mult),
                    reads=[xname, "mv", f"rstd{u}"], writes=["xn_tm"])
                P.emit("vector", lambda v, u=u, xb=xb: v.tensor_scalar(
                    xa[:, ts_(u, 1024)], xb[:, ts_(u, 1024)], sc[:, 6 + u:7 + u],
                    sc[:, 4 + u:5 + u], op0=ALU.mult, op1=ALU.add),
                    reads=[xname, f"s1_{u}", f"nmr{u}"], writes=["xa"])
                transpose_pe(xn_tm[:, ts_(u, 1024)], "xn_tm",
                             xn_fm[:, ts_(u, 1024)], "xn_fm",
                             psB[:, ts_(u, 1024)], "pB")

            proj_fm("k", xn_fm, None, "xn_fm")
            evac_fm("bk", Kf, "Kf")
            proj_tm("v", xn_fm, "xn_fm")
            evac_tm_rep("bv", Vf, "Vf")
            proj_fm("q", xn_fm, None, "xn_fm")
            evac_fm("bq", Qf, "Qf")

            proj_tm("kp", Kf, "Kf")
            evac_tm_rep("bkp", t0, "t0")
            trig_tm()
            P.emit("vector", lambda v: v.tensor_mul(br_tm[:], Vf[:], cosb[:]),
                   reads=["Vf", "cosb"], writes=["br_tm"])
            P.emit("vector", lambda v: v.tensor_mul(bi_tm[:], Vf[:], sinb[:]),
                   reads=["Vf", "sinb"], writes=["bi_tm"])

            proj_tm("qp", Qf, "Qf")
            evac_tm_rep("bqp", t0, "t0")
            trig_tm()

            for u in range(2):
                for (srcb, srcn, off) in ((br_tm, "br_tm", 0), (bi_tm, "bi_tm", 1024)):
                    for h2 in range(2):
                        first = (off == 0 and h2 == 0)
                        last = (off == 1024 and h2 == 1)
                        P.emit("tensor", lambda t, u=u, srcb=srcb, off=off, h2=h2: t.matmul(
                            psB[:, off + h2 * 512: off + h2 * 512 + 512], u64_sb[:],
                            srcb[:, u * 1024 + h2 * 512: u * 1024 + h2 * 512 + 512],
                            start=True, stop=True),
                            reads=["u64", srcn],
                            writes=["pB"] if (first or last) else [])
                P.emit("vector", lambda v, u=u: v.tensor_mul(
                    r1, psB[:, 0:1024], cosb[:, ts_(u, 1024)]),
                    reads=["pB", "cosb"], writes=["th"])
                P.emit("vector", lambda v, u=u: v.tensor_mul(
                    r2, psB[:, 1024:2048], sinb[:, ts_(u, 1024)]),
                    reads=["pB", "sinb"], writes=["t_x"])
                P.emit("vector", lambda v, u=u: v.tensor_add(
                    retr[:], r1, r2),
                    reads=["th", "t_x"], writes=["retr"])
                P.emit("vector", lambda v: v.bn_stats(bn6b2[:, 0:6], retr[:, 0:512]),
                       reads=["retr"], writes=["bn6a2"])
                P.emit("vector", lambda v: v.bn_stats(bn6b2[:, 6:12], retr[:, 512:1024]),
                       reads=["retr"], writes=["bn6b2"])
                P.emit("vector", lambda v: v.bn_aggr(mv2[:, 0:2], bn6b2[:]),
                       reads=["bn6a2", "bn6b2"], writes=["mv2"])
                P.emit("vector", lambda v: v.tensor_scalar_add(sc2[:, 0:1], mv2[:, 1:2], EPS),
                       reads=["mv2"], writes=["sc20"])
                emit_rsqrt(sc2[:, 0:1], "sc20", sc2[:, 2:3], "rstd2", sc2[:, 1:2], "sc21")
                P.emit("vector", lambda v: v.tensor_scalar(
                    z_tm[:], retr[:], mv2[:, 0:1], sc2[:, 2:3],
                    op0=ALU.subtract, op1=ALU.mult),
                    reads=["retr", "mv2", "rstd2"], writes=["z_tm"])
                transpose_pe(z_tm[:], "z_tm", z_fm[:, ts_(u, 1024)], "z_fm", pA0, "pA0")
                for h2 in range(2):
                    for c in range(NC8):
                        first = (h2 == 0 and c == 0)
                        P.emit("tensor", lambda t, u=u, h2=h2, c=c: t.matmul(
                            pA1[:, h2 * 512:h2 * 512 + 512],
                            z_fm[:, u * 1024 + c * 128: u * 1024 + c * 128 + 128],
                            W_sb["o"][:, (c * 8 + h2 * 4) * 128:(c * 8 + h2 * 4) * 128 + 512],
                            start=(c == 0), stop=False),
                            reads=["z_fm", "W_o"],
                            writes=["pA1"] if first else [])
                    P.emit("tensor", lambda t, h2=h2: t.matmul(
                        pA1[:, h2 * 512:h2 * 512 + 512], ones1b[:],
                        bo_bf[:, h2 * 512:(h2 + 1) * 512], start=False, stop=True),
                        reads=["ones1b", "bo_bf"],
                        writes=["pA1"] if h2 == 1 else [])
                P.emit("vector", lambda v, u=u: v.scalar_tensor_tensor(
                    t_x[:], pA1[:, 0:1024], 1.0, xa[:, ts_(u, 1024)],
                    op0=ALU.mult, op1=ALU.add),
                    reads=["pA1", "xa"], writes=["t_x"])
                P.emit("sync", lambda s, i=i, u=u: s.dma_start(
                    xmid_h[i * TA + u * 128: i * TA + (u + 1) * 128, :], t_x[:]),
                    reads=["t_x"], writes=["xmid_done"], dma=True)

        if debug_stop == "wdump":
            wdump_h = nc.dram_tensor("wdump", [6, 128, 8192], BF16, kind="ExternalOutput").ap()
            for k, n in enumerate(("k", "v", "q", "kp", "qp", "o")):
                P.emit("gpsimd", lambda g, k=k, n=n: g.dma_start(wdump_h[k], W_sb[n][:]),
                       reads=["W_" + n, "xmid_done"], writes=["wdump_done"], dma=True)
        if debug_stop is not None:
            P.emit("gpsimd", lambda g: g.engine_nop(), reads=["xmid_done", "wdump_done"] if debug_stop == "wdump" else ["xmid_done"])
            a_stack.close()
            P.build()
            return nc

        snap = dict(P.cnt)
        for e in P.ENGINES:
            waits = tuple((o, snap[o]) for o in P.ENGINES if o != e and snap[o] > 0)
            P.cnt[e] += 1
            P.ops[e].append((waits, lambda eng: eng.nop(), 1))
        a_stack.close()

        f_stack = ExitStack()

        def sbf(name, shape, dtype):
            return f_stack.enter_context(nc.sbuf_tensor(name, shape, dtype))

        f1_sb = sbf("f1_sb", [128, 8 * ES * FF], BF16)
        D_sb = sbf("D_sb", [128, 8 * ES * 32], BF16)
        Ir = sbf("Ir", [128, 32 * 512], BF16)
        hi_tm = sbf("hi_tm", [128, 8192], BF16)
        lo_ts = sbf("lo_ts", [128, D], BF16)
        t1f = sbf("t1f", [128, D], F32)
        xbf = [sbf(f"xbf{i}", [128, D], F32) for i in range(2)]
        tie_sb = sbf("tie_sb", [128, TF], BF16)
        mb = sbf("mb", [128, 32], F32)
        perm_fm = sbf("perm_fm", [128, TF], BF16)
        perm_tm = sbf("perm_tm", [128, 8 * ES], BF16)
        chi = sbf("chi", [128, 8 * 128], BF16)
        clo = sbf("clo", [128, 8 * 128], BF16)
        biasq2 = sbf("biasq2_sb", [32, 128], F32)
        fbn6 = sbf("fbn6", [128, 12], F32)
        fmv = sbf("fmv", [128, 2], F32)
        fsc = sbf("fsc", [128, 4], F32)
        fident = sbf("fident", [128, 128], BF16)
        pexp = sbf("pexp", [128, 512], BF16)
        hi_fm = D_sb[:, 0:8192]
        lo_fm = D_sb[:, 8192:16384]
        logit_sb = D_sb[:, 16384:18432].bitcast(F32)
        innerf = D_sb[0:32, 16384:24576].bitcast(F32)
        f2blk = [D_sb[0:32, 8192:16384], D_sb[0:32, 16384:24576]]
        stgbig = D_sb[0:32, 24576:32768]
        f2st = D_sb[0:32, 4096:12288].bitcast(F32)
        inner = D_sb[0:32, 0:4096]
        sctf = t1f[0:32, 0:0 + 1024]

        P.emit("vector", lambda v: v.memset(pexp[:], 0.0), writes=["pexp"])
        P.emit("gpsimd", lambda g: g.dma_start(xbf[0][:, 0:128], ident_h),
               writes=["xbf0"], dma=True)
        P.emit("vector", lambda v: v.tensor_copy(fident[:], xbf[0][:, 0:128]),
               reads=["xbf0"], writes=["fident"])
        for c in range(NC8):
            for j in range(4):
                sl = slice(j * 1024, (j + 1) * 1024)
                P.emit("gpsimd", lambda g, c=c, sl=sl: g.dma_start(
                    xbf[1][:], f1_h[c, :, sl]), writes=["xbf1"], dma=True)
                P.emit("vector", lambda v, c=c, sl=sl: v.tensor_copy(
                    f1_sb[:, c * 4096 + sl.start: c * 4096 + sl.stop], xbf[1][:]),
                    reads=["xbf1"], writes=["f1_sb"])
        for c in range(NC8):
            P.emit("gpsimd", lambda g, c=c: g.dma_start(
                xbf[0][:, 0:128], ctrl_h[ts_(c, 128), :]), writes=["xbf0"], dma=True)
            P.emit("vector", lambda v, c=c: v.tensor_copy(
                chi[:, ts_(c, 128)], xbf[0][:, 0:128]), reads=["xbf0"], writes=["chi"])
            P.emit("vector", lambda v, c=c: v.tensor_sub(
                clo[:, ts_(c, 128)], xbf[0][:, 0:128], chi[:, ts_(c, 128)]),
                reads=["xbf0", "chi"], writes=["clo"])
        P.emit("gpsimd", lambda g: g.dma_start(biasq2[:], biasq2_h),
               writes=["biasq"], dma=True)
        P.emit("gpsimd", lambda g: g.dma_start(xbf[0][:], tie_h[:, 0:1024]),
               writes=["xbf0"], dma=True)
        P.emit("vector", lambda v: v.tensor_copy(tie_sb[:], xbf[0][:]),
               reads=["xbf0"], writes=["tie_sb"])
        for j in range(32):
            sl = slice(j * 4096, (j + 1) * 4096)
            P.emit("gpsimd", lambda g, sl=sl: g.dma_start(f2st, f2q2_h[:, sl]),
                   writes=["D"], dma=True)
            P.emit("vector", lambda v: v.tensor_copy(stgbig[:, 0:4096], f2st),
                   reads=["D"], writes=["D"])
            P.emit("gpsimd", lambda g, sl=sl: g.dma_start(f2bf_h[:, sl], stgbig[:, 0:4096]),
                   reads=["D"], writes=["f2bf"], dma=True)

        def transpose_f(src_ap, srcname, dst_ap, dstname, slot_ap, slotname):
            slotb = slot_ap.bitcast(BF16)
            for c in range(NC8):
                P.emit("tensor", lambda t, c=c, slotb=slotb, src_ap=src_ap: t.transpose(
                    slotb[:, c * 128:(c + 1) * 128], src_ap[:, c * 128:(c + 1) * 128],
                    fident[:]),
                    reads=[srcname, "fident"],
                    writes=[slotname] if c in (0, NC8 - 1) else [])
            P.emit("vector", lambda v, slotb=slotb, dst_ap=dst_ap: v.tensor_copy(
                dst_ap, slotb[:, 0:1024]),
                reads=[slotname], writes=[dstname])

        NQ_EMIT = 1 if f_stop else NQF
        for qq in range(NQ_EMIT):
            for w in range(8):
                xb = xbf[w % 2]
                xbn = f"xbf{w % 2}"
                row0 = qq * TF + w * 128
                P.emit("gpsimd", lambda g, row0=row0, xb=xb: g.dma_start(
                    xb[:], xmid_h[row0:row0 + 128, :]),
                    reads=["xmid_done"], writes=[xbn], dma=True)
                P.emit("sync", lambda g, row0=row0, xb=xb: g.dma_start(
                    out_h[row0:row0 + 128, :], xb[:]),
                    reads=[xbn], writes=["out_pre"], dma=True)
                P.emit("vector", lambda v, xb=xb: v.bn_stats(fbn6[:, 0:6], xb[:, 0:512]),
                       reads=[xbn], writes=["fbn6a"])
                P.emit("vector", lambda v, xb=xb: v.bn_stats(fbn6[:, 6:12], xb[:, 512:1024]),
                       reads=[xbn], writes=["fbn6b"])
                P.emit("vector", lambda v: v.bn_aggr(fmv[:], fbn6[:]),
                       reads=["fbn6a", "fbn6b"], writes=["fmv"])
                P.emit("vector", lambda v: v.tensor_scalar_add(fsc[:, 0:1], fmv[:, 1:2], EPS),
                       reads=["fmv"], writes=["fsc0"])
                emit_rsqrt(fsc[:, 0:1], "fsc0", fsc[:, 2:3], "frstd", fsc[:, 1:2], "fsc1")
                P.emit("vector", lambda v, xb=xb: v.tensor_scalar(
                    t1f[:], xb[:], fmv[:, 0:1], fsc[:, 2:3],
                    op0=ALU.subtract, op1=ALU.mult),
                    reads=[xbn, "fmv", "frstd"], writes=["t1f"])
                P.emit("vector", lambda v, w=w: v.tensor_copy(
                    hi_tm[:, ts_(w, 1024)], t1f[:]),
                    reads=["t1f"], writes=["hi_tm"])
                P.emit("vector", lambda v, w=w: v.tensor_sub(
                    lo_ts[:], t1f[:], hi_tm[:, ts_(w, 1024)]),
                    reads=["t1f", "hi_tm"], writes=["lo_ts"])
                transpose_f(hi_tm[:, ts_(w, 1024)], "hi_tm",
                            hi_fm[:, ts_(w, 1024)], "D", pA0, "pA0")
                transpose_f(lo_ts[:], "lo_ts",
                            lo_fm[:, ts_(w, 1024)], "D", pA0, "pA0")

            if f_stop == "ln3":
                break
            terms = ((chi, "chi", hi_fm), (clo, "clo", hi_fm), (chi, "chi", lo_fm))
            for w in range(8):
                k = 0
                for (cw, cwn, act) in terms:
                    for c in range(NC8):
                        first = (k == 0)
                        last = (k == 3 * NC8 - 1)
                        P.emit("tensor", lambda t, w=w, c=c, cw=cw, act=act, first=first, last=last: t.matmul(
                            psB[:, w * 128:(w + 1) * 128],
                            cw[:, ts_(c, 128)],
                            act[:, w * 1024 + c * 128: w * 1024 + c * 128 + 128],
                            start=first, stop=last),
                            reads=[cwn, "D"],
                            writes=["pB"] if ((first and w == 0) or (last and w == 7)) else [])
                        k += 1
            P.emit("vector", lambda v: v.tensor_add(logit_sb, psB[:, 0:1024], tie_sb[:]),
                   reads=["pB", "tie_sb"], writes=["D"])
            P.emit("vector", lambda v: v.reduce_max(
                mb[:], logit_sb.rearrange("p (a b) -> p a b", b=32), axis=AX.X),
                reads=["D"], writes=["mb"])
            P.emit("vector", lambda v: v.tensor_tensor(
                perm_fm[:].rearrange("p (a b) -> p a b", b=32),
                logit_sb.rearrange("p (a b) -> p a b", b=32),
                mb[:].unsqueeze(2).broadcast_to([128, 32, 32]), op=ALU.is_equal),
                reads=["D", "mb"], writes=["perm_fm"])
            slotb = pA0.bitcast(BF16)
            for w in range(8):
                P.emit("tensor", lambda t, w=w, slotb=slotb: t.transpose(
                    slotb[:, w * 128:(w + 1) * 128], perm_fm[:, w * 128:(w + 1) * 128],
                    fident[:]),
                    reads=["perm_fm", "fident"],
                    writes=["pA0"] if w in (0, 7) else [])
            P.emit("vector", lambda v, slotb=slotb: v.tensor_copy(perm_tm[:], slotb[:, 0:1024]),
                   reads=["pA0"], writes=["perm_tm"])

            if f_stop == "perm":
                break
            for w in range(8):
                for gi in range(4):
                    P.emit("vector", lambda v, w=w, gi=gi: v.tensor_copy(
                        pexp[32 * gi:32 * gi + 32, gi * 128:(gi + 1) * 128],
                        perm_tm[32 * gi:32 * gi + 32, w * 128:(w + 1) * 128]),
                        reads=["perm_tm"], writes=["pexp"])
                for c in range(NC8):
                    slot = pA1 if ((w * 8 + c) % 2) else pA0
                    slotn = "pA1" if ((w * 8 + c) % 2) else "pA0"
                    P.emit("tensor", lambda t, w=w, c=c, slot=slot: t.matmul(
                        slot[:, 0:512],
                        hi_tm[:, w * 1024 + c * 128: w * 1024 + (c + 1) * 128],
                        pexp[:],
                        start=True, stop=True),
                        reads=["hi_tm", "pexp"],
                        writes=[slotn])
                    src = slot[:, 0:512].rearrange("p (g e) -> p e g", g=4)
                    dst = D_sb[:, c * 4096:(c + 1) * 4096].rearrange(
                        "p (e g) -> p e g", g=32)[:, :, 4 * w:4 * w + 4]
                    P.emit("scalar", lambda sc_, src=src, dst=dst: sc_.copy(dst, src),
                           reads=[slotn], writes=["D"])

            if f_stop == "dispatch":
                break
            def f1_region(e):
                if e < 32:
                    return pA0[0:32, e * 32:(e + 1) * 32], "pA0"
                if e < 64:
                    return pA1[0:32, (e - 64 + 32) * 32:(e - 32 + 1) * 32] if False else pA1[0:32, (e - 32) * 32:(e - 31) * 32], "pA1"
                return psB[0:32, (e - 64) * 32:(e - 63) * 32], "pB"
            for e in range(ES):
                reg, regn = f1_region(e)
                for c in range(NC8):
                    first = (c == 0 and e in (0, 32, 64))
                    last = (c == NC8 - 1 and e in (31, 63, 127))
                    P.emit("tensor", lambda t, e=e, c=c, reg=reg: t.matmul(
                        reg,
                        f1_sb[:, (c * ES + e) * FF:(c * ES + e) * FF + FF],
                        D_sb[:, c * 4096 + e * 32: c * 4096 + e * 32 + 32],
                        start=(c == 0), stop=(c == NC8 - 1)),
                        reads=["f1_sb", "D"],
                        writes=[regn] if (first or last) else [])
            for (reg, regn, e0, ne) in ((pA0[0:32, :], "pA0", 0, 32), (pA1[0:32, :], "pA1", 32, 32), (psB[0:32, 0:2048], "pB", 64, 64)):
                bb = biasq2[:, e0:e0 + ne].unsqueeze(2).broadcast_to([32, ne, 32])
                P.emit("vector", lambda v, reg=reg, e0=e0, ne=ne, bb=bb: v.tensor_tensor(
                    innerf[:, 0:ne * 32].rearrange("p (e g) -> p e g", g=32),
                    reg.rearrange("p (e g) -> p e g", g=32), bb, op=ALU.add),
                    reads=[regn, "biasq"], writes=["D"])
                P.emit("vector", lambda v, e0=e0, ne=ne: v.tensor_scalar_max(
                    inner[:, e0 * 32:(e0 + ne) * 32], innerf[:, 0:ne * 32], 0.0),
                    reads=["D"], writes=["D"])

            if f_stop == "f1":
                break
            for pb_ in (32, 64, 96):
                P.emit("vector", lambda v, pb_=pb_: v.memset(D_sb[pb_:pb_ + 32, 0:4096], 0.0),
                       reads=[], writes=["Din", "D"])
                P.emit("vector", lambda v, pb_=pb_: v.memset(D_sb[pb_:pb_ + 32, 8192:24576], 0.0),
                       reads=[], writes=["Df2a", "Df2b", "D"])
            for hd in range(2):
                for eb in range(8):
                    fb = f2blk[eb % 2]
                    fbF = D_sb[:, 8192 + (eb % 2) * 8192: 8192 + (eb % 2) * 8192 + 8192]
                    fbn_ = "Df2a" if (eb % 2 == 0) else "Df2b"
                    innerF = D_sb[:, 0:4096]
                    P.emit("gpsimd", lambda g, eb=eb, hd=hd, fbF=fbF: g.dma_start(
                        fbF[0:32, :].rearrange("p (e d) -> p e d", d=512),
                        f2bf_h[:, eb * 16384:(eb + 1) * 16384].rearrange(
                            "p (e d) -> p e d", d=1024)[:, :, hd * 512:hd * 512 + 512]),
                        reads=["f2bf"], writes=[fbn_, "D"], dma=True)
                    for ei in range(16):
                        e = eb * 16 + ei
                        slot = pA0 if (e % 2 == 0) else pA1
                        slotn = "pA0" if (e % 2 == 0) else "pA1"
                        P.emit("tensor", lambda t, e=e, ei=ei, fbF=fbF, slot=slot: t.matmul(
                            slot[0:32, 0:512],
                            innerF[:, e * 32:(e + 1) * 32],
                            fbF[:, ei * 512:(ei + 1) * 512],
                            start=True, stop=True),
                            reads=["Din", fbn_],
                            writes=[slotn])
                        P.emit("scalar", lambda sc_, ei=ei, slot=slot: sc_.copy(
                            stgbig[:, ei * 512:(ei + 1) * 512], slot[0:32, 0:512]),
                            reads=[slotn], writes=["Dstg"])
                    rlh = rl_h if (eb % 2 == 0) else rl2_h
                    rln = "rl" if (eb % 2 == 0) else "rl2"
                    P.emit("gpsimd", lambda g, rlh=rlh: g.dma_start(rlh, stgbig[:, 0:8192]),
                           reads=["Dstg"], writes=[rln], dma=True)
                    src = rlh.rearrange("g (e d) -> e g d", d=512)
                    dst = Ir[eb * 16:(eb + 1) * 16, :].rearrange("p (g d) -> p g d", d=512)
                    P.emit("gpsimd", lambda g, src=src, dst=dst: g.dma_start(dst, src),
                           reads=[rln], writes=["Ir"], dma=True)
                if f_stop == "f2":
                    continue
                for w in range(8):
                    for gi in range(4):
                        gg = 4 * w + gi
                        P.emit("tensor", lambda t, w=w, gi=gi, gg=gg: t.matmul(
                            psB[0:32, gi * 512:(gi + 1) * 512],
                            perm_fm[:, w * 128 + 32 * gi: w * 128 + 32 * gi + 32],
                            Ir[:, gg * 512:(gg + 1) * 512],
                            start=True, stop=True),
                            reads=["perm_fm", "Ir"],
                            writes=["pB"] if gi in (0, 3) else [])
                    P.emit("vector", lambda v: v.tensor_copy(t1f[0:32, 0:1024], psB[0:32, 0:1024]),
                           reads=["pB"], writes=["t1f"])
                    P.emit("vector", lambda v: v.tensor_copy(t1f[0:32, 0:0 + 1024], t1f[0:32, 0:1024]) if False else v.tensor_copy(lo_ts[0:32, 0:1024].bitcast(F32)[:, 0:512], psB[0:32, 1024:1536]),
                           reads=["pB"], writes=["lo_ts"])
                    P.emit("vector", lambda v: v.tensor_copy(lo_ts[0:32, 0:1024].bitcast(F32)[:, 512:1024] if False else xbf[0][0:32, 512:1024], psB[0:32, 1536:2048]),
                           reads=["pB"], writes=["xbf0"])
                    row0 = qq * TF + w * 128
                    P.emit("gpsimd", lambda g, row0=row0, hd=hd: g.dma_start(
                        out_h[row0:row0 + 64, hd * 512:hd * 512 + 512].rearrange(
                            "(g p) d -> p g d", p=32),
                        t1f[0:32, 0:1024].rearrange("p (g d) -> p g d", d=512),
                        accum_op=ALU.add),
                        reads=["t1f", "out_pre"], writes=["out_done"], dma=True)
                    P.emit("gpsimd", lambda g, row0=row0, hd=hd: g.dma_start(
                        out_h[row0 + 64:row0 + 96, hd * 512:hd * 512 + 512],
                        lo_ts[0:32, 0:1024].bitcast(F32)[:, 0:512],
                        accum_op=ALU.add),
                        reads=["lo_ts", "out_pre"], writes=["out_done"], dma=True)
                    P.emit("gpsimd", lambda g, row0=row0, hd=hd: g.dma_start(
                        out_h[row0 + 96:row0 + 128, hd * 512:hd * 512 + 512],
                        xbf[0][0:32, 512:1024],
                        accum_op=ALU.add),
                        reads=["xbf0", "out_pre"], writes=["out_done"], dma=True)

        if f_stop is None:
            P.emit("gpsimd", lambda g: g.nop(), reads=["out_done"])
        else:
            allb = [b for b in ("out_done", "Ir", "D", "perm_tm", "hi_tm") if b in P.WR]
            for b in allb:
                P.emit("gpsimd", lambda g: g.nop(), reads=[b])
        f_stack.close()
        P.build()
        return nc


def _host_inputs(inp):
    x = np.ascontiguousarray(np.asarray(inp["L0_attn_x"], dtype=np.float32).reshape(16 * 2048, 1024))
    def v8(a):
        return np.ascontiguousarray(np.asarray(a, dtype=np.float32).reshape(8, 128).T)
    base = {
        "Wk": np.ascontiguousarray(inp["L0_attn_Wk"], dtype=np.float32),
        "Wv": np.ascontiguousarray(inp["L0_attn_Wv"], dtype=np.float32),
        "Wq": np.ascontiguousarray(inp["L0_attn_Wq"], dtype=np.float32),
        "Wkp": np.ascontiguousarray(inp["L0_attn_Wkp"], dtype=np.float32),
        "Wqp": np.ascontiguousarray(inp["L0_attn_Wqp"], dtype=np.float32),
        "Wo": np.ascontiguousarray(inp["L0_attn_Wo"], dtype=np.float32),
        "bk": v8(inp["L0_attn_bk"]), "bv": v8(inp["L0_attn_bv"]), "bq": v8(inp["L0_attn_bq"]),
        "bkp": v8(inp["L0_attn_bkp"]), "bqp": v8(inp["L0_attn_bqp"]),
        "ps": v8(inp["L0_attn_phase_scale"]),
        "lng": v8(inp["L0_attn_ln_g"]), "lnb": v8(inp["L0_attn_ln_b"]),
        "bo": np.ascontiguousarray(np.asarray(inp["L0_attn_bo"], dtype=np.float32).reshape(1, 1024)),
    }
    k = np.arange(128)
    u64 = ((k[:, None] // 64 == k[None, :] // 64) & (k[:, None] <= k[None, :]))
    base["u64"] = u64.astype(np.float32)
    base["ident"] = np.eye(128, dtype=np.float32)
    base["bvr"] = np.ascontiguousarray(np.broadcast_to(
        np.asarray(inp["L0_attn_bv"], dtype=np.float32)[None, :], (128, 1024)))
    base["bkpr"] = np.ascontiguousarray(np.broadcast_to(
        np.asarray(inp["L0_attn_bkp"], dtype=np.float32)[None, :], (128, 1024)))
    base["bqpr"] = np.ascontiguousarray(np.broadcast_to(
        np.asarray(inp["L0_attn_bqp"], dtype=np.float32)[None, :], (128, 1024)))
    base["psr"] = np.ascontiguousarray(np.broadcast_to(
        np.asarray(inp["L0_attn_phase_scale"], dtype=np.float32)[None, :], (128, 1024)))
    tie = np.linspace(0.0, 1e-6, 32, dtype=np.float32)
    base["tie"] = np.ascontiguousarray(np.broadcast_to(np.tile(tie, TF // 32)[None, :], (128, TF)))
    ctrl = np.ascontiguousarray(np.asarray(inp["L0_ffn_controller"], dtype=np.float32).reshape(1024, 128))
    f1 = np.ascontiguousarray(np.asarray(inp["L0_ffn_f1"], dtype=np.float32).reshape(8, 128, 128 * 32))
    fb = np.asarray(inp["L0_ffn_bias"], dtype=np.float32).reshape(128, 32)
    f2 = np.asarray(inp["L0_ffn_f2"], dtype=np.float32).reshape(128, 32, 1024)
    e = np.arange(128)
    r_, q_ = e // 32, e % 32
    biasq = np.zeros((128, 32), dtype=np.float32)
    f2q = np.zeros((128, 32 * 1024), dtype=np.float32)
    for ei in range(128):
        rr, qq = ei // 32, ei % 32
        biasq[32 * rr: 32 * rr + 32, qq] = fb[ei]
        f2q[32 * rr: 32 * rr + 32, qq * 1024:(qq + 1) * 1024] = f2[ei]
    base["ctrl"] = ctrl
    base["f1"] = f1
    base["biasq"] = np.ascontiguousarray(biasq)
    base["f2q"] = np.ascontiguousarray(f2q)
    base["biasq2"] = np.ascontiguousarray(fb.T)
    base["f2q2"] = np.ascontiguousarray(np.transpose(f2, (1, 0, 2)).reshape(32, 128 * 1024))

    in_maps = []
    for i in range(8):
        m = dict(base)
        m["x"] = np.ascontiguousarray(x[i * T:(i + 1) * T])
        in_maps.append(m)
    return in_maps


_DEBUG_STOP = None


def kernel(**inputs):
    in_maps = _host_inputs(inputs)
    nc = build_nc(debug_stop=_DEBUG_STOP)
    res = run_bass_kernel_spmd(nc, in_maps, core_ids=list(range(NCORES)))
    key = "xmid" if _DEBUG_STOP == "attn" else "out"
    out = np.concatenate([np.asarray(r[key]) for r in res.results], axis=0)
    return out.reshape(16, 2048, 1024)


# version 24
# speedup vs baseline: 1.1893x; 1.1117x over previous
import numpy as np
import ml_dtypes
from contextlib import ExitStack
import concourse.bass as bass
import concourse.mybir as mybir
from concourse.bass_utils import run_bass_kernel_spmd

F32 = mybir.dt.float32
BF16 = mybir.dt.bfloat16
AF = mybir.ActivationFunctionType
ALU = mybir.AluOpType
AX = mybir.AxisListType

NCORES = 8
T = 4096
D = 1024
NC8 = 8
TA = 256
NTA = T // TA
TF = 1024
NQF = T // TF
ES = 128
FF = 32
EPS = 1e-5
PI = float(np.pi)


class Prog:
    ENGINES = ("sync", "tensor", "vector", "scalar", "gpsimd")

    def __init__(self, nc, stack):
        self.nc = nc
        self.ops = {e: [] for e in self.ENGINES}
        self.sems = {}
        self.cnt = {}
        for e in self.ENGINES:
            self.sems[e] = stack.enter_context(nc.semaphore("sem_" + e))
            self.cnt[e] = 0
        self.WR = {}
        self.LR = {}
        self._dma_cnt = {}
        self._last_tp = None

    DMA_ENGINES = ("sync", "gpsimd", "scalar")

    def emit(self, eng, fn, reads=(), writes=(), dma=False, transpose=False):
        waits = []
        for b in reads:
            waits += self.WR.get(b, [])
        for b in writes:
            waits += self.LR.pop(b, [])
            waits += self.WR.get(b, [])
        amt = 16 if dma else 1
        self.cnt[eng] += amt
        ev = (eng, self.cnt[eng])
        wmax = {}
        for (s, v) in waits:
            if wmax.get(s, -1) < v:
                wmax[s] = v
        self.ops[eng].append((tuple(wmax.items()), fn, amt))
        for b in reads:
            self.LR.setdefault(b, []).append(ev)
        for b in writes:
            self.WR[b] = [ev]
        return ev

    def barrier_event(self, bufs):
        evs = []
        for b in bufs:
            evs += self.WR.get(b, []) + self.LR.get(b, [])
        return evs

    def build(self):
        nc = self.nc
        with nc.Block() as block:
            for e in self.ENGINES:
                ops = self.ops[e]
                sems = self.sems
                mysem = sems[e]

                def body(engine, ops=ops, e=e):
                    last = {}
                    for waits, fn, amt in ops:
                        for (s, v) in waits:
                            if last.get(s, 0) >= v:
                                continue
                            engine.wait_ge(sems[s], v)
                            last[s] = v
                        inst = fn(engine)
                        inst.then_inc(mysem, amt)
                getattr(block, e)(body)


def ts_(i, n):
    return slice(i * n, (i + 1) * n)


def build_nc(debug_stop=None, f_stop=None):
    nc = bass.Bass("TRN2", target_bir_lowering=False)
    dt_in = {}

    def din(name, shape, dtype=F32):
        t = nc.dram_tensor(name, shape, dtype, kind="ExternalInput").ap()
        dt_in[name] = t
        return t

    x_h = din("x", [T, D])
    Wk_h = din("Wk", [D, D]); Wv_h = din("Wv", [D, D]); Wq_h = din("Wq", [D, D])
    Wkp_h = din("Wkp", [D, D]); Wqp_h = din("Wqp", [D, D]); Wo_h = din("Wo", [D, D])
    bk_h = din("bk", [128, 8]); bv_h = din("bv", [128, 8]); bq_h = din("bq", [128, 8])
    bkp_h = din("bkp", [128, 8]); bqp_h = din("bqp", [128, 8])
    ps_h = din("ps", [128, 8]); lng_h = din("lng", [128, 8]); lnb_h = din("lnb", [128, 8])
    bo_h = din("bo", [1, D])
    u64_h = din("u64", [128, 128])
    ident_h = din("ident", [128, 128])
    bvr_h = din("bvr", [128, D]); bkpr_h = din("bkpr", [128, D]); bqpr_h = din("bqpr", [128, D])
    psr_h = din("psr", [128, D])
    ctrl_h = din("ctrl", [D, ES])
    f1_h = din("f1", [NC8, 128, ES * FF])
    f2q_h = din("f2q", [128, 32 * D])
    biasq_h = din("biasq", [128, 32])
    f2q2_h = din("f2q2", [32, 128 * D])
    biasq2_h = din("biasq2", [32, 128])
    tie_h = din("tie", [128, TF])

    out_h = nc.dram_tensor("out", [T, D], F32, kind="ExternalOutput").ap()
    if debug_stop is not None:
        xmid_h = nc.dram_tensor("xmid", [T, D], F32, kind="ExternalOutput").ap()
        dbg_h = nc.dram_tensor("dbg", [T, D], F32, kind="ExternalOutput").ap()
    else:
        xmid_h = nc.dram_tensor("xmid", [T, D], F32, kind="Internal").ap()
        dbg_h = None
    f2bf_h = nc.dram_tensor("f2bf", [32, 128 * D], BF16, kind="Internal").ap()
    rl_h = nc.dram_tensor("rl", [32, 8192], BF16, kind="Internal").ap()
    rl2_h = nc.dram_tensor("rl2", [32, 8192], BF16, kind="Internal").ap()

    with ExitStack() as stack:
        P = Prog(nc, stack)

        def sb(name, shape, dtype):
            return stack.enter_context(nc.sbuf_tensor(name, shape, dtype))


        a_stack = ExitStack()

        def sba(name, shape, dtype):
            return a_stack.enter_context(nc.sbuf_tensor(name, shape, dtype))

        W_sb = {n: sba("W_" + n, [128, 8192], BF16) for n in ("k", "v", "q", "kp", "qp", "o")}
        vec_sb = {n: sba("vec_" + n, [128, 8], F32)
                  for n in ("bk", "bv", "bq", "bkp", "bqp", "ps", "aps", "lng", "lnb")}
        u64_sb = sba("u64_sb", [128, 128], BF16)
        cpihalf = sba("cpihalf", [128, 1], F32)
        wtmp = [sba(f"wtmp{i}", [128, 1024], F32) for i in range(2)]
        wtmpb = sba("wtmpb", [128, 1024], BF16)
        bo_row = wtmp[0][0:1, 0:1024]

        xbuf = [sba(f"x{i}", [128, TA * 8], F32) for i in range(2)]
        xa = sba("xa", [128, TA * 8], F32)
        xn_tm = sba("xn_tm", [128, TA * 8], BF16)
        xn_fm = sba("xn_fm", [128, TA * 8], BF16)
        Kf = sba("Kf", [128, TA * 8], BF16)
        Qf = sba("Qf", [128, TA * 8], BF16)
        Vf = sba("Vf", [128, TA * 8], BF16)
        t0 = sba("t0", [128, TA * 8], F32)
        th = sba("th", [128, TA * 8], F32)
        sinb = sba("sinb", [128, TA * 8], BF16)
        cosb = sba("cosb", [128, TA * 8], BF16)
        br_tm = sba("br_tm", [128, TA * 8], BF16)
        bi_tm = sba("bi_tm", [128, TA * 8], BF16)
        rep = {n: sba("rep_" + n, [128, D], BF16) for n in ("bv", "bkp", "bqp", "ps")}
        ident_sb = sba("ident_sb", [128, 128], BF16)
        retr = sba("retr", [128, D], BF16)
        z_tm = sba("z_tm", [128, D], BF16)
        z_fm = sba("z_fm", [128, TA * 8], BF16)
        t_x = sba("t_x", [128, D], F32)
        r1 = th[:, 0:1024]
        r2 = t_x[:]
        bn6 = sba("bn6", [128, 12], F32)
        mv = sba("mv", [128, 4], F32)
        sc = sba("sc", [128, 8], F32)
        sc2 = sba("sc2", [128, 4], F32)
        bn6b2 = sba("bn6b2", [128, 12], F32)
        mv2 = sba("mv2", [128, 2], F32)
        psA = stack.enter_context(nc.psum_tensor("psA", [128, 2048], F32))
        psB = stack.enter_context(nc.psum_tensor("psB", [128, 2048], F32))

        I32 = mybir.dt.int32

        def emit_rsqrt(tv_ap, tv_name, out_ap, out_name, scr_ap, scr_name):
            ti = tv_ap.bitcast(I32)
            oi = out_ap.bitcast(I32)
            P.emit("vector", lambda v: v.tensor_single_scalar(oi, ti, 1, op=ALU.logical_shift_right),
                   reads=[tv_name], writes=[out_name])
            P.emit("vector", lambda v: v.tensor_scalar(oi, oi, -1, 0x5F3759DF, op0=ALU.mult, op1=ALU.add),
                   reads=[out_name], writes=[out_name])
            for _ in range(2):
                P.emit("vector", lambda v: v.tensor_mul(scr_ap, out_ap, out_ap),
                       reads=[out_name], writes=[scr_name])
                P.emit("vector", lambda v: v.tensor_scalar(scr_ap, scr_ap, -0.5, None, op0=ALU.mult),
                       reads=[scr_name], writes=[scr_name])
                P.emit("vector", lambda v: v.tensor_mul(scr_ap, scr_ap, tv_ap),
                       reads=[scr_name, tv_name], writes=[scr_name])
                P.emit("vector", lambda v: v.tensor_scalar_add(scr_ap, scr_ap, 1.5),
                       reads=[scr_name], writes=[scr_name])
                P.emit("vector", lambda v: v.tensor_mul(out_ap, out_ap, scr_ap),
                       reads=[scr_name, out_name], writes=[out_name])

        for n, h in (("k", Wk_h), ("v", Wv_h), ("q", Wq_h), ("kp", Wkp_h), ("qp", Wqp_h)):
            for c in range(NC8):
                buf = wtmp[c % 2]
                bname = f"wtmp{c % 2}"
                P.emit("gpsimd", lambda g, buf=buf, h=h, c=c: g.dma_start(buf[:], h[ts_(c, 128), :]),
                       writes=[bname], dma=True)
                P.emit("vector", lambda v, n=n, c=c, buf=buf: v.tensor_copy(
                    W_sb[n][:, ts_(c, 1024)], buf[:]),
                    reads=[bname], writes=["W_" + n])
        for n, h in (("bk", bk_h), ("bv", bv_h), ("bq", bq_h), ("bkp", bkp_h),
                     ("bqp", bqp_h), ("ps", ps_h), ("lng", lng_h), ("lnb", lnb_h)):
            P.emit("gpsimd", lambda s, n=n, h=h: s.dma_start(vec_sb[n][:], h),
                   writes=["vec_" + n], dma=True)
        for (tname, hsrc, dstap, dwr) in (
                ("u64", u64_h, u64_sb[:], "u64"), ("ident", ident_h, ident_sb[:], "ident")):
            P.emit("gpsimd", lambda g, hsrc=hsrc: g.dma_start(wtmp[0][:, 0:128], hsrc),
                   writes=["wtmp0"], dma=True)
            P.emit("vector", lambda v, dstap=dstap: v.tensor_copy(dstap, wtmp[0][:, 0:128]),
                   reads=["wtmp0"], writes=[dwr])
        for n, h in (("bv", bvr_h), ("bkp", bkpr_h), ("bqp", bqpr_h), ("ps", psr_h)):
            P.emit("gpsimd", lambda g, h=h: g.dma_start(wtmp[1][:], h),
                   writes=["wtmp1"], dma=True)
            P.emit("vector", lambda v, n=n: v.tensor_copy(rep[n][:], wtmp[1][:]),
                   reads=["wtmp1"], writes=["rep_" + n])
        P.emit("scalar", lambda s: s.activation(vec_sb["aps"][:], vec_sb["ps"][:], AF.Tanh),
               reads=["vec_ps"], writes=["vec_aps"])
        P.emit("scalar", lambda s: s.activation(vec_sb["aps"][:], vec_sb["ps"][:], AF.Abs),
               reads=["vec_ps"], writes=["vec_aps"])
        P.emit("vector", lambda v: v.memset(cpihalf[:], PI / 2), writes=["cpihalf"])
        for c in range(NC8):
            buf = wtmp[c % 2]
            bname = f"wtmp{c % 2}"
            src = Wo_h[ts_(c, 128), :]
            P.emit("gpsimd", lambda s, buf=buf, src=src: s.dma_start(buf[:], src),
                   reads=[], writes=[bname], dma=True)
            P.emit("vector", lambda v, c=c, buf=buf: v.tensor_scalar_mul(
                W_sb["o"][:, ts_(c, 1024)], buf[:], vec_sb["lng"][:, c:c + 1]),
                reads=[bname, "vec_lng"], writes=["W_o"])
            for h2 in range(2):
                first = (c == 0 and h2 == 0)
                last = (c == NC8 - 1 and h2 == 1)
                P.emit("tensor", lambda t, c=c, h2=h2, buf=buf: t.matmul(
                    psA[0:1, h2 * 512:(h2 + 1) * 512], vec_sb["lnb"][:, c:c + 1],
                    buf[:, h2 * 512:(h2 + 1) * 512], start=(c == 0), stop=(c == NC8 - 1)),
                    reads=[bname, "vec_lnb"], writes=["psA"] if (first or last) else [])
        P.emit("gpsimd", lambda s: s.dma_start(bo_row, bo_h), writes=["wtmp0"], dma=True)
        P.emit("vector", lambda v: v.tensor_add(bo_row, bo_row, psA[0:1, 0:1024]),
               reads=["psA", "wtmp0"], writes=["wtmp0"])
        ones1b = sba("ones1b", [1, 128], BF16)
        bo_bf = sba("bo_bf", [1, D], BF16)
        P.emit("vector", lambda v: v.memset(ones1b[:], 1.0), writes=["ones1b"])
        P.emit("vector", lambda v: v.tensor_copy(bo_bf[:], bo_row),
               reads=["wtmp0"], writes=["bo_bf"])


        pA0 = psA[:, 0:1024]
        pA1 = psA[:, 1024:2048]

        def proj_fm(wname, in_fm, pslots, inbuf):
            in3 = in_fm.rearrange("p (u w) -> p u w", u=2)
            for j in range(8):
                for c in range(NC8):
                    first = (c == 0 and j == 0)
                    last = (c == NC8 - 1 and j == 7)
                    P.emit("tensor", lambda t, j=j, c=c, in3=in3: t.matmul(
                        psA[:, j * TA:(j + 1) * TA].rearrange("p (u w) -> p u w", u=2),
                        W_sb[wname][:, (c * 8 + j) * 128:(c * 8 + j) * 128 + 128],
                        in3[:, :, c * 128:(c + 1) * 128],
                        start=(c == 0), stop=(c == NC8 - 1)),
                        reads=["W_" + wname, inbuf],
                        writes=["pA0", "pA1"] if (first or last) else [])

        def evac_fm(biasvec, outbuf, outname):
            for j in range(8):
                P.emit("scalar", lambda sc_, j=j: sc_.add(
                    outbuf.rearrange("p (u w) -> p u w", u=2)[:, :, j * 128:(j + 1) * 128],
                    psA[:, j * TA:(j + 1) * TA].rearrange("p (u w) -> p u w", u=2),
                    vec_sb[biasvec][:, j:j + 1]),
                    reads=["pA0", "pA1", "vec_" + biasvec], writes=[outname])

        def proj_tm(wname, lhs_fm, lhsname):
            for u in range(2):
                for h in range(2):
                    for c in range(NC8):
                        first = (u == 0 and h == 0 and c == 0)
                        last = (u == 1 and h == 1 and c == NC8 - 1)
                        P.emit("tensor", lambda t, u=u, h=h, c=c: t.matmul(
                            psB[:, u * 1024 + h * 512: u * 1024 + h * 512 + 512],
                            lhs_fm[:, u * 1024 + c * 128: u * 1024 + c * 128 + 128],
                            W_sb[wname][:, (c * 8 + h * 4) * 128:(c * 8 + h * 4) * 128 + 512],
                            start=(c == 0), stop=(c == NC8 - 1)),
                            reads=["W_" + wname, lhsname],
                            writes=["pB"] if (first or last) else [])

        def evac_tm_rep(repname, outbuf, outname, dtype_note=None):
            for u in range(2):
                P.emit("vector", lambda v, u=u: v.tensor_add(
                    outbuf[:, ts_(u, 1024)], psB[:, ts_(u, 1024)], rep[repname][:]),
                    reads=["pB", "rep_" + repname], writes=[outname])

        def trig_tm():
            P.emit("scalar", lambda s: s.activation(th[:], t0[:], AF.Tanh),
                   reads=["t0"], writes=["th"])
            P.emit("vector", lambda v: v.tensor_mul(
                t0[:].rearrange("p (u w) -> p u w", u=2), th[:].rearrange("p (u w) -> p u w", u=2),
                rep["ps"][:].unsqueeze(1).broadcast_to([128, 2, 1024])),
                reads=["th", "rep_ps"], writes=["t0"])
            P.emit("scalar", lambda s: s.activation(sinb[:], t0[:], AF.Sin),
                   reads=["t0"], writes=["sinb"])
            P.emit("scalar", lambda s: s.activation(th[:], t0[:], AF.Abs),
                   reads=["t0"], writes=["th"])
            P.emit("scalar", lambda s: s.activation(cosb[:], th[:], AF.Sin,
                   bias=cpihalf[:], scale=-1.0),
                   reads=["th", "cpihalf"], writes=["cosb"])

        def transpose_pe(src_ap, srcname, dst_ap, dstname, slot_ap, slotname):
            slotb = slot_ap.bitcast(BF16)
            for c in range(NC8):
                P.emit("tensor", lambda t, c=c, slotb=slotb: t.transpose(
                    slotb[:, c * 128:(c + 1) * 128], src_ap[:, c * 128:(c + 1) * 128],
                    ident_sb[:]),
                    reads=[srcname, "ident"],
                    writes=[slotname] if c in (0, NC8 - 1) else [])
            P.emit("vector", lambda v, slotb=slotb: v.tensor_copy(dst_ap, slotb[:, 0:1024]),
                   reads=[slotname], writes=[dstname])

        for i in range(NTA):
            xb = xbuf[i % 2]
            xname = f"x{i % 2}"
            P.emit("gpsimd", lambda s, i=i, xb=xb: s.dma_start(
                xb[:].rearrange("p (a q) -> p a q", q=1024),
                x_h[ts_(i, TA), :].rearrange("(a p) q -> p a q", p=128)),
                writes=[xname], dma=True)

            for u in range(2):
                P.emit("vector", lambda v, u=u, xb=xb: v.bn_stats(
                    bn6[:, 0:6], xb[:, u * 1024:u * 1024 + 512]),
                    reads=[xname], writes=["bn6a"])
                P.emit("vector", lambda v, u=u, xb=xb: v.bn_stats(
                    bn6[:, 6:12], xb[:, u * 1024 + 512:u * 1024 + 1024]),
                    reads=[xname], writes=["bn6b"])
                P.emit("vector", lambda v, u=u: v.bn_aggr(mv[:, 2 * u:2 * u + 2], bn6[:]),
                       reads=["bn6a", "bn6b"], writes=["mv"])
                P.emit("vector", lambda v, u=u: v.tensor_scalar_add(
                    sc[:, 0:1], mv[:, 2 * u + 1:2 * u + 2], EPS), reads=["mv"], writes=["sc0"])
                emit_rsqrt(sc[:, 0:1], "sc0", sc[:, 2 + u:3 + u], f"rstd{u}", sc[:, 1:2], "sc1")
                P.emit("vector", lambda v, u=u: v.tensor_scalar(
                    sc[:, 4 + u:5 + u], mv[:, 2 * u:2 * u + 1], sc[:, 2 + u:3 + u], -1.0,
                    op0=ALU.mult, op1=ALU.mult), reads=["mv", f"rstd{u}"], writes=[f"nmr{u}"])
                P.emit("vector", lambda v, u=u: v.tensor_scalar_add(
                    sc[:, 6 + u:7 + u], sc[:, 2 + u:3 + u], 1.0),
                    reads=[f"rstd{u}"], writes=[f"s1_{u}"])
                P.emit("vector", lambda v, u=u, xb=xb: v.tensor_scalar(
                    xn_tm[:, ts_(u, 1024)], xb[:, ts_(u, 1024)],
                    mv[:, 2 * u:2 * u + 1], sc[:, 2 + u:3 + u],
                    op0=ALU.subtract, op1=ALU.mult),
                    reads=[xname, "mv", f"rstd{u}"], writes=["xn_tm"])
                P.emit("vector", lambda v, u=u, xb=xb: v.tensor_scalar(
                    xa[:, ts_(u, 1024)], xb[:, ts_(u, 1024)], sc[:, 6 + u:7 + u],
                    sc[:, 4 + u:5 + u], op0=ALU.mult, op1=ALU.add),
                    reads=[xname, f"s1_{u}", f"nmr{u}"], writes=["xa"])
                transpose_pe(xn_tm[:, ts_(u, 1024)], "xn_tm",
                             xn_fm[:, ts_(u, 1024)], "xn_fm",
                             psB[:, ts_(u, 1024)], "pB")

            proj_fm("k", xn_fm, None, "xn_fm")
            evac_fm("bk", Kf, "Kf")
            proj_tm("v", xn_fm, "xn_fm")
            evac_tm_rep("bv", Vf, "Vf")
            proj_fm("q", xn_fm, None, "xn_fm")
            evac_fm("bq", Qf, "Qf")

            proj_tm("kp", Kf, "Kf")
            evac_tm_rep("bkp", t0, "t0")
            trig_tm()
            P.emit("vector", lambda v: v.tensor_mul(br_tm[:], Vf[:], cosb[:]),
                   reads=["Vf", "cosb"], writes=["br_tm"])
            P.emit("vector", lambda v: v.tensor_mul(bi_tm[:], Vf[:], sinb[:]),
                   reads=["Vf", "sinb"], writes=["bi_tm"])

            proj_tm("qp", Qf, "Qf")
            evac_tm_rep("bqp", t0, "t0")
            trig_tm()

            for u in range(2):
                for (srcb, srcn, off) in ((br_tm, "br_tm", 0), (bi_tm, "bi_tm", 1024)):
                    for h2 in range(2):
                        first = (off == 0 and h2 == 0)
                        last = (off == 1024 and h2 == 1)
                        P.emit("tensor", lambda t, u=u, srcb=srcb, off=off, h2=h2: t.matmul(
                            psB[:, off + h2 * 512: off + h2 * 512 + 512], u64_sb[:],
                            srcb[:, u * 1024 + h2 * 512: u * 1024 + h2 * 512 + 512],
                            start=True, stop=True),
                            reads=["u64", srcn],
                            writes=["pB"] if (first or last) else [])
                P.emit("vector", lambda v, u=u: v.tensor_mul(
                    r1, psB[:, 0:1024], cosb[:, ts_(u, 1024)]),
                    reads=["pB", "cosb"], writes=["th"])
                P.emit("vector", lambda v, u=u: v.tensor_mul(
                    r2, psB[:, 1024:2048], sinb[:, ts_(u, 1024)]),
                    reads=["pB", "sinb"], writes=["t_x"])
                P.emit("vector", lambda v, u=u: v.tensor_add(
                    retr[:], r1, r2),
                    reads=["th", "t_x"], writes=["retr"])
                P.emit("vector", lambda v: v.bn_stats(bn6b2[:, 0:6], retr[:, 0:512]),
                       reads=["retr"], writes=["bn6a2"])
                P.emit("vector", lambda v: v.bn_stats(bn6b2[:, 6:12], retr[:, 512:1024]),
                       reads=["retr"], writes=["bn6b2"])
                P.emit("vector", lambda v: v.bn_aggr(mv2[:, 0:2], bn6b2[:]),
                       reads=["bn6a2", "bn6b2"], writes=["mv2"])
                P.emit("vector", lambda v: v.tensor_scalar_add(sc2[:, 0:1], mv2[:, 1:2], EPS),
                       reads=["mv2"], writes=["sc20"])
                emit_rsqrt(sc2[:, 0:1], "sc20", sc2[:, 2:3], "rstd2", sc2[:, 1:2], "sc21")
                P.emit("vector", lambda v: v.tensor_scalar(
                    z_tm[:], retr[:], mv2[:, 0:1], sc2[:, 2:3],
                    op0=ALU.subtract, op1=ALU.mult),
                    reads=["retr", "mv2", "rstd2"], writes=["z_tm"])
                transpose_pe(z_tm[:], "z_tm", z_fm[:, ts_(u, 1024)], "z_fm", pA0, "pA0")
                for h2 in range(2):
                    for c in range(NC8):
                        first = (h2 == 0 and c == 0)
                        P.emit("tensor", lambda t, u=u, h2=h2, c=c: t.matmul(
                            pA1[:, h2 * 512:h2 * 512 + 512],
                            z_fm[:, u * 1024 + c * 128: u * 1024 + c * 128 + 128],
                            W_sb["o"][:, (c * 8 + h2 * 4) * 128:(c * 8 + h2 * 4) * 128 + 512],
                            start=(c == 0), stop=False),
                            reads=["z_fm", "W_o"],
                            writes=["pA1"] if first else [])
                    P.emit("tensor", lambda t, h2=h2: t.matmul(
                        pA1[:, h2 * 512:h2 * 512 + 512], ones1b[:],
                        bo_bf[:, h2 * 512:(h2 + 1) * 512], start=False, stop=True),
                        reads=["ones1b", "bo_bf"],
                        writes=["pA1"] if h2 == 1 else [])
                P.emit("vector", lambda v, u=u: v.scalar_tensor_tensor(
                    t_x[:], pA1[:, 0:1024], 1.0, xa[:, ts_(u, 1024)],
                    op0=ALU.mult, op1=ALU.add),
                    reads=["pA1", "xa"], writes=["t_x"])
                P.emit("sync", lambda s, i=i, u=u: s.dma_start(
                    xmid_h[i * TA + u * 128: i * TA + (u + 1) * 128, :], t_x[:]),
                    reads=["t_x"], writes=["xmid_done"], dma=True)

        if debug_stop == "wdump":
            wdump_h = nc.dram_tensor("wdump", [6, 128, 8192], BF16, kind="ExternalOutput").ap()
            for k, n in enumerate(("k", "v", "q", "kp", "qp", "o")):
                P.emit("gpsimd", lambda g, k=k, n=n: g.dma_start(wdump_h[k], W_sb[n][:]),
                       reads=["W_" + n, "xmid_done"], writes=["wdump_done"], dma=True)
        if debug_stop is not None:
            P.emit("gpsimd", lambda g: g.engine_nop(), reads=["xmid_done", "wdump_done"] if debug_stop == "wdump" else ["xmid_done"])
            a_stack.close()
            P.build()
            return nc

        snap = dict(P.cnt)
        for e in P.ENGINES:
            waits = tuple((o, snap[o]) for o in P.ENGINES if o != e and snap[o] > 0)
            P.cnt[e] += 1
            P.ops[e].append((waits, lambda eng: eng.nop(), 1))
        a_stack.close()

        f_stack = ExitStack()

        def sbf(name, shape, dtype):
            return f_stack.enter_context(nc.sbuf_tensor(name, shape, dtype))

        f1_sb = sbf("f1_sb", [128, 8 * ES * FF], BF16)
        D_sb = sbf("D_sb", [128, 8 * ES * 32], BF16)
        Ir = sbf("Ir", [128, 32 * 512], BF16)
        hi_tm = sbf("hi_tm", [128, 8192], BF16)
        lo_ts = sbf("lo_ts", [128, D], BF16)
        t1f = sbf("t1f", [128, D], F32)
        xbf = [sbf(f"xbf{i}", [128, D], F32) for i in range(2)]
        tie_sb = sbf("tie_sb", [128, TF], BF16)
        mb = sbf("mb", [128, 32], F32)
        perm_fm = sbf("perm_fm", [128, TF], BF16)
        perm_tm = sbf("perm_tm", [128, 8 * ES], BF16)
        chi = sbf("chi", [128, 8 * 128], BF16)
        clo = sbf("clo", [128, 8 * 128], BF16)
        biasq2 = sbf("biasq2_sb", [32, 128], F32)
        fbn6 = sbf("fbn6", [128, 12], F32)
        fmv = sbf("fmv", [128, 2], F32)
        fsc = sbf("fsc", [128, 4], F32)
        fident = sbf("fident", [128, 128], BF16)
        pexp = sbf("pexp", [128, 512], BF16)
        hi_fm = D_sb[:, 0:8192]
        lo_fm = D_sb[:, 8192:16384]
        logit_sb = D_sb[:, 16384:18432].bitcast(F32)
        innerf = D_sb[0:32, 16384:24576].bitcast(F32)
        f2blk = [D_sb[0:32, 8192:16384], D_sb[0:32, 16384:24576]]
        stgbig = D_sb[0:32, 24576:32768]
        f2st = D_sb[0:32, 4096:12288].bitcast(F32)
        inner = D_sb[0:32, 0:4096]
        sctf = t1f[0:32, 0:0 + 1024]

        P.emit("vector", lambda v: v.memset(pexp[:], 0.0), writes=["pexp"])
        P.emit("gpsimd", lambda g: g.dma_start(xbf[0][:, 0:128], ident_h),
               writes=["xbf0"], dma=True)
        P.emit("vector", lambda v: v.tensor_copy(fident[:], xbf[0][:, 0:128]),
               reads=["xbf0"], writes=["fident"])
        for c in range(NC8):
            for j in range(4):
                sl = slice(j * 1024, (j + 1) * 1024)
                P.emit("gpsimd", lambda g, c=c, sl=sl: g.dma_start(
                    xbf[1][:], f1_h[c, :, sl]), writes=["xbf1"], dma=True)
                P.emit("vector", lambda v, c=c, sl=sl: v.tensor_copy(
                    f1_sb[:, c * 4096 + sl.start: c * 4096 + sl.stop], xbf[1][:]),
                    reads=["xbf1"], writes=["f1_sb"])
        for c in range(NC8):
            P.emit("gpsimd", lambda g, c=c: g.dma_start(
                xbf[0][:, 0:128], ctrl_h[ts_(c, 128), :]), writes=["xbf0"], dma=True)
            P.emit("vector", lambda v, c=c: v.tensor_copy(
                chi[:, ts_(c, 128)], xbf[0][:, 0:128]), reads=["xbf0"], writes=["chi"])
            P.emit("vector", lambda v, c=c: v.tensor_sub(
                clo[:, ts_(c, 128)], xbf[0][:, 0:128], chi[:, ts_(c, 128)]),
                reads=["xbf0", "chi"], writes=["clo"])
        P.emit("gpsimd", lambda g: g.dma_start(biasq2[:], biasq2_h),
               writes=["biasq"], dma=True)
        P.emit("gpsimd", lambda g: g.dma_start(xbf[0][:], tie_h[:, 0:1024]),
               writes=["xbf0"], dma=True)
        P.emit("vector", lambda v: v.tensor_copy(tie_sb[:], xbf[0][:]),
               reads=["xbf0"], writes=["tie_sb"])
        for j in range(32):
            sl = slice(j * 4096, (j + 1) * 4096)
            P.emit("gpsimd", lambda g, sl=sl: g.dma_start(f2st, f2q2_h[:, sl]),
                   writes=["D"], dma=True)
            P.emit("vector", lambda v: v.tensor_copy(stgbig[:, 0:4096], f2st),
                   reads=["D"], writes=["D"])
            P.emit("gpsimd", lambda g, sl=sl: g.dma_start(f2bf_h[:, sl], stgbig[:, 0:4096]),
                   reads=["D"], writes=["f2bf"], dma=True)

        def transpose_f(src_ap, srcname, dst_ap, dstname, slot_ap, slotname):
            slotb = slot_ap.bitcast(BF16)
            for c in range(NC8):
                P.emit("tensor", lambda t, c=c, slotb=slotb, src_ap=src_ap: t.transpose(
                    slotb[:, c * 128:(c + 1) * 128], src_ap[:, c * 128:(c + 1) * 128],
                    fident[:]),
                    reads=[srcname, "fident"],
                    writes=[slotname] if c in (0, NC8 - 1) else [])
            P.emit("vector", lambda v, slotb=slotb, dst_ap=dst_ap: v.tensor_copy(
                dst_ap, slotb[:, 0:1024]),
                reads=[slotname], writes=[dstname])

        NQ_EMIT = 1 if f_stop else NQF
        for qq in range(NQ_EMIT):
            for w in range(8):
                xb = xbf[w % 2]
                xbn = f"xbf{w % 2}"
                row0 = qq * TF + w * 128
                P.emit("scalar", lambda g, row0=row0, xb=xb: g.dma_start(
                    xb[:], xmid_h[row0:row0 + 128, :]),
                    reads=["xmid_done"], writes=[xbn], dma=True)
                P.emit("sync", lambda g, row0=row0, xb=xb: g.dma_start(
                    out_h[row0:row0 + 128, :], xb[:]),
                    reads=[xbn], writes=["out_pre"], dma=True)
                P.emit("vector", lambda v, xb=xb: v.bn_stats(fbn6[:, 0:6], xb[:, 0:512]),
                       reads=[xbn], writes=["fbn6a"])
                P.emit("vector", lambda v, xb=xb: v.bn_stats(fbn6[:, 6:12], xb[:, 512:1024]),
                       reads=[xbn], writes=["fbn6b"])
                P.emit("vector", lambda v: v.bn_aggr(fmv[:], fbn6[:]),
                       reads=["fbn6a", "fbn6b"], writes=["fmv"])
                P.emit("vector", lambda v: v.tensor_scalar_add(fsc[:, 0:1], fmv[:, 1:2], EPS),
                       reads=["fmv"], writes=["fsc0"])
                emit_rsqrt(fsc[:, 0:1], "fsc0", fsc[:, 2:3], "frstd", fsc[:, 1:2], "fsc1")
                P.emit("vector", lambda v, xb=xb: v.tensor_scalar(
                    t1f[:], xb[:], fmv[:, 0:1], fsc[:, 2:3],
                    op0=ALU.subtract, op1=ALU.mult),
                    reads=[xbn, "fmv", "frstd"], writes=["t1f"])
                P.emit("vector", lambda v, w=w: v.tensor_copy(
                    hi_tm[:, ts_(w, 1024)], t1f[:]),
                    reads=["t1f"], writes=["hi_tm"])
                P.emit("vector", lambda v, w=w: v.tensor_sub(
                    lo_ts[:], t1f[:], hi_tm[:, ts_(w, 1024)]),
                    reads=["t1f", "hi_tm"], writes=["lo_ts"])
                transpose_f(hi_tm[:, ts_(w, 1024)], "hi_tm",
                            hi_fm[:, ts_(w, 1024)], "D", pA0, "pA0")
                transpose_f(lo_ts[:], "lo_ts",
                            lo_fm[:, ts_(w, 1024)], "D", pA0, "pA0")

            if f_stop == "ln3":
                break
            terms = ((chi, "chi", hi_fm), (clo, "clo", hi_fm), (chi, "chi", lo_fm))
            for w in range(8):
                k = 0
                for (cw, cwn, act) in terms:
                    for c in range(NC8):
                        first = (k == 0)
                        last = (k == 3 * NC8 - 1)
                        P.emit("tensor", lambda t, w=w, c=c, cw=cw, act=act, first=first, last=last: t.matmul(
                            psB[:, w * 128:(w + 1) * 128],
                            cw[:, ts_(c, 128)],
                            act[:, w * 1024 + c * 128: w * 1024 + c * 128 + 128],
                            start=first, stop=last),
                            reads=[cwn, "D"],
                            writes=["pB"] if ((first and w == 0) or (last and w == 7)) else [])
                        k += 1
            P.emit("vector", lambda v: v.tensor_add(logit_sb, psB[:, 0:1024], tie_sb[:]),
                   reads=["pB", "tie_sb"], writes=["D"])
            P.emit("vector", lambda v: v.reduce_max(
                mb[:], logit_sb.rearrange("p (a b) -> p a b", b=32), axis=AX.X),
                reads=["D"], writes=["mb"])
            P.emit("vector", lambda v: v.tensor_tensor(
                perm_fm[:].rearrange("p (a b) -> p a b", b=32),
                logit_sb.rearrange("p (a b) -> p a b", b=32),
                mb[:].unsqueeze(2).broadcast_to([128, 32, 32]), op=ALU.is_equal),
                reads=["D", "mb"], writes=["perm_fm"])
            slotb = pA0.bitcast(BF16)
            for w in range(8):
                P.emit("tensor", lambda t, w=w, slotb=slotb: t.transpose(
                    slotb[:, w * 128:(w + 1) * 128], perm_fm[:, w * 128:(w + 1) * 128],
                    fident[:]),
                    reads=["perm_fm", "fident"],
                    writes=["pA0"] if w in (0, 7) else [])
            P.emit("vector", lambda v, slotb=slotb: v.tensor_copy(perm_tm[:], slotb[:, 0:1024]),
                   reads=["pA0"], writes=["perm_tm"])

            if f_stop == "perm":
                break
            for w in range(8):
                for gi in range(4):
                    P.emit("vector", lambda v, w=w, gi=gi: v.tensor_copy(
                        pexp[32 * gi:32 * gi + 32, gi * 128:(gi + 1) * 128],
                        perm_tm[32 * gi:32 * gi + 32, w * 128:(w + 1) * 128]),
                        reads=["perm_tm"], writes=["pexp"])
                for c in range(NC8):
                    slot = pA1 if ((w * 8 + c) % 2) else pA0
                    slotn = "pA1" if ((w * 8 + c) % 2) else "pA0"
                    P.emit("tensor", lambda t, w=w, c=c, slot=slot: t.matmul(
                        slot[:, 0:512],
                        hi_tm[:, w * 1024 + c * 128: w * 1024 + (c + 1) * 128],
                        pexp[:],
                        start=True, stop=True),
                        reads=["hi_tm", "pexp"],
                        writes=[slotn])
                    src = slot[:, 0:512].rearrange("p (g e) -> p e g", g=4)
                    dst = D_sb[:, c * 4096:(c + 1) * 4096].rearrange(
                        "p (e g) -> p e g", g=32)[:, :, 4 * w:4 * w + 4]
                    P.emit("scalar", lambda sc_, src=src, dst=dst: sc_.copy(dst, src),
                           reads=[slotn], writes=["D"])

            if f_stop == "dispatch":
                break
            def f1_region(e):
                if e < 32:
                    return pA0[0:32, e * 32:(e + 1) * 32], "pA0"
                if e < 64:
                    return pA1[0:32, (e - 64 + 32) * 32:(e - 32 + 1) * 32] if False else pA1[0:32, (e - 32) * 32:(e - 31) * 32], "pA1"
                return psB[0:32, (e - 64) * 32:(e - 63) * 32], "pB"
            for e in range(ES):
                reg, regn = f1_region(e)
                for c in range(NC8):
                    first = (c == 0 and e in (0, 32, 64))
                    last = (c == NC8 - 1 and e in (31, 63, 127))
                    P.emit("tensor", lambda t, e=e, c=c, reg=reg: t.matmul(
                        reg,
                        f1_sb[:, (c * ES + e) * FF:(c * ES + e) * FF + FF],
                        D_sb[:, c * 4096 + e * 32: c * 4096 + e * 32 + 32],
                        start=(c == 0), stop=(c == NC8 - 1)),
                        reads=["f1_sb", "D"],
                        writes=[regn] if (first or last) else [])
            for (reg, regn, e0, ne) in ((pA0[0:32, :], "pA0", 0, 32), (pA1[0:32, :], "pA1", 32, 32), (psB[0:32, 0:2048], "pB", 64, 64)):
                bb = biasq2[:, e0:e0 + ne].unsqueeze(2).broadcast_to([32, ne, 32])
                P.emit("vector", lambda v, reg=reg, e0=e0, ne=ne, bb=bb: v.tensor_tensor(
                    innerf[:, 0:ne * 32].rearrange("p (e g) -> p e g", g=32),
                    reg.rearrange("p (e g) -> p e g", g=32), bb, op=ALU.add),
                    reads=[regn, "biasq"], writes=["D"])
                P.emit("vector", lambda v, e0=e0, ne=ne: v.tensor_scalar_max(
                    inner[:, e0 * 32:(e0 + ne) * 32], innerf[:, 0:ne * 32], 0.0),
                    reads=["D"], writes=["D"])

            if f_stop == "f1":
                break
            for pb_ in (32, 64, 96):
                P.emit("vector", lambda v, pb_=pb_: v.memset(D_sb[pb_:pb_ + 32, 0:4096], 0.0),
                       reads=[], writes=["Din", "D"])
                P.emit("vector", lambda v, pb_=pb_: v.memset(D_sb[pb_:pb_ + 32, 8192:24576], 0.0),
                       reads=[], writes=["Df2a", "Df2b", "D"])
            for hd in range(2):
                for eb in range(8):
                    fb = f2blk[eb % 2]
                    fbF = D_sb[:, 8192 + (eb % 2) * 8192: 8192 + (eb % 2) * 8192 + 8192]
                    fbn_ = "Df2a" if (eb % 2 == 0) else "Df2b"
                    innerF = D_sb[:, 0:4096]
                    P.emit("sync", lambda g, eb=eb, hd=hd, fbF=fbF: g.dma_start(
                        fbF[0:32, :].rearrange("p (e d) -> p e d", d=512),
                        f2bf_h[:, eb * 16384:(eb + 1) * 16384].rearrange(
                            "p (e d) -> p e d", d=1024)[:, :, hd * 512:hd * 512 + 512]),
                        reads=["f2bf"], writes=[fbn_, "D"], dma=True)
                    for ei in range(16):
                        e = eb * 16 + ei
                        slot = pA0 if (e % 2 == 0) else pA1
                        slotn = "pA0" if (e % 2 == 0) else "pA1"
                        P.emit("tensor", lambda t, e=e, ei=ei, fbF=fbF, slot=slot: t.matmul(
                            slot[0:32, 0:512],
                            innerF[:, e * 32:(e + 1) * 32],
                            fbF[:, ei * 512:(ei + 1) * 512],
                            start=True, stop=True),
                            reads=["Din", fbn_],
                            writes=[slotn])
                        P.emit("scalar", lambda sc_, ei=ei, slot=slot: sc_.copy(
                            stgbig[:, ei * 512:(ei + 1) * 512], slot[0:32, 0:512]),
                            reads=[slotn], writes=["Dstg"])
                    rlh = rl_h if (eb % 2 == 0) else rl2_h
                    rln = "rl" if (eb % 2 == 0) else "rl2"
                    P.emit("gpsimd", lambda g, rlh=rlh: g.dma_start(rlh, stgbig[:, 0:8192]),
                           reads=["Dstg"], writes=[rln], dma=True)
                    src = rlh.rearrange("g (e d) -> e g d", d=512)
                    dst = Ir[eb * 16:(eb + 1) * 16, :].rearrange("p (g d) -> p g d", d=512)
                    P.emit("gpsimd", lambda g, src=src, dst=dst: g.dma_start(dst, src),
                           reads=[rln], writes=["Ir"], dma=True)
                if f_stop == "f2":
                    continue
                for w in range(8):
                    for gi in range(4):
                        gg = 4 * w + gi
                        P.emit("tensor", lambda t, w=w, gi=gi, gg=gg: t.matmul(
                            psB[0:32, gi * 512:(gi + 1) * 512],
                            perm_fm[:, w * 128 + 32 * gi: w * 128 + 32 * gi + 32],
                            Ir[:, gg * 512:(gg + 1) * 512],
                            start=True, stop=True),
                            reads=["perm_fm", "Ir"],
                            writes=["pB"] if gi in (0, 3) else [])
                    P.emit("vector", lambda v: v.tensor_copy(t1f[0:32, 0:1024], psB[0:32, 0:1024]),
                           reads=["pB"], writes=["t1f"])
                    P.emit("vector", lambda v: v.tensor_copy(t1f[0:32, 0:0 + 1024], t1f[0:32, 0:1024]) if False else v.tensor_copy(lo_ts[0:32, 0:1024].bitcast(F32)[:, 0:512], psB[0:32, 1024:1536]),
                           reads=["pB"], writes=["lo_ts"])
                    P.emit("vector", lambda v: v.tensor_copy(lo_ts[0:32, 0:1024].bitcast(F32)[:, 512:1024] if False else xbf[0][0:32, 512:1024], psB[0:32, 1536:2048]),
                           reads=["pB"], writes=["xbf0"])
                    row0 = qq * TF + w * 128
                    P.emit("gpsimd", lambda g, row0=row0, hd=hd: g.dma_start(
                        out_h[row0:row0 + 64, hd * 512:hd * 512 + 512].rearrange(
                            "(g p) d -> p g d", p=32),
                        t1f[0:32, 0:1024].rearrange("p (g d) -> p g d", d=512),
                        accum_op=ALU.add),
                        reads=["t1f", "out_pre"], writes=["out_done"], dma=True)
                    P.emit("gpsimd", lambda g, row0=row0, hd=hd: g.dma_start(
                        out_h[row0 + 64:row0 + 96, hd * 512:hd * 512 + 512],
                        lo_ts[0:32, 0:1024].bitcast(F32)[:, 0:512],
                        accum_op=ALU.add),
                        reads=["lo_ts", "out_pre"], writes=["out_done"], dma=True)
                    P.emit("gpsimd", lambda g, row0=row0, hd=hd: g.dma_start(
                        out_h[row0 + 96:row0 + 128, hd * 512:hd * 512 + 512],
                        xbf[0][0:32, 512:1024],
                        accum_op=ALU.add),
                        reads=["xbf0", "out_pre"], writes=["out_done"], dma=True)

        if f_stop is None:
            P.emit("gpsimd", lambda g: g.nop(), reads=["out_done"])
        else:
            allb = [b for b in ("out_done", "Ir", "D", "perm_tm", "hi_tm") if b in P.WR]
            for b in allb:
                P.emit("gpsimd", lambda g: g.nop(), reads=[b])
        f_stack.close()
        P.build()
        return nc


def _host_inputs(inp):
    x = np.ascontiguousarray(np.asarray(inp["L0_attn_x"], dtype=np.float32).reshape(16 * 2048, 1024))
    def v8(a):
        return np.ascontiguousarray(np.asarray(a, dtype=np.float32).reshape(8, 128).T)
    base = {
        "Wk": np.ascontiguousarray(inp["L0_attn_Wk"], dtype=np.float32),
        "Wv": np.ascontiguousarray(inp["L0_attn_Wv"], dtype=np.float32),
        "Wq": np.ascontiguousarray(inp["L0_attn_Wq"], dtype=np.float32),
        "Wkp": np.ascontiguousarray(inp["L0_attn_Wkp"], dtype=np.float32),
        "Wqp": np.ascontiguousarray(inp["L0_attn_Wqp"], dtype=np.float32),
        "Wo": np.ascontiguousarray(inp["L0_attn_Wo"], dtype=np.float32),
        "bk": v8(inp["L0_attn_bk"]), "bv": v8(inp["L0_attn_bv"]), "bq": v8(inp["L0_attn_bq"]),
        "bkp": v8(inp["L0_attn_bkp"]), "bqp": v8(inp["L0_attn_bqp"]),
        "ps": v8(inp["L0_attn_phase_scale"]),
        "lng": v8(inp["L0_attn_ln_g"]), "lnb": v8(inp["L0_attn_ln_b"]),
        "bo": np.ascontiguousarray(np.asarray(inp["L0_attn_bo"], dtype=np.float32).reshape(1, 1024)),
    }
    k = np.arange(128)
    u64 = ((k[:, None] // 64 == k[None, :] // 64) & (k[:, None] <= k[None, :]))
    base["u64"] = u64.astype(np.float32)
    base["ident"] = np.eye(128, dtype=np.float32)
    base["bvr"] = np.ascontiguousarray(np.broadcast_to(
        np.asarray(inp["L0_attn_bv"], dtype=np.float32)[None, :], (128, 1024)))
    base["bkpr"] = np.ascontiguousarray(np.broadcast_to(
        np.asarray(inp["L0_attn_bkp"], dtype=np.float32)[None, :], (128, 1024)))
    base["bqpr"] = np.ascontiguousarray(np.broadcast_to(
        np.asarray(inp["L0_attn_bqp"], dtype=np.float32)[None, :], (128, 1024)))
    base["psr"] = np.ascontiguousarray(np.broadcast_to(
        np.asarray(inp["L0_attn_phase_scale"], dtype=np.float32)[None, :], (128, 1024)))
    tie = np.linspace(0.0, 1e-6, 32, dtype=np.float32)
    base["tie"] = np.ascontiguousarray(np.broadcast_to(np.tile(tie, TF // 32)[None, :], (128, TF)))
    ctrl = np.ascontiguousarray(np.asarray(inp["L0_ffn_controller"], dtype=np.float32).reshape(1024, 128))
    f1 = np.ascontiguousarray(np.asarray(inp["L0_ffn_f1"], dtype=np.float32).reshape(8, 128, 128 * 32))
    fb = np.asarray(inp["L0_ffn_bias"], dtype=np.float32).reshape(128, 32)
    f2 = np.asarray(inp["L0_ffn_f2"], dtype=np.float32).reshape(128, 32, 1024)
    e = np.arange(128)
    r_, q_ = e // 32, e % 32
    biasq = np.zeros((128, 32), dtype=np.float32)
    f2q = np.zeros((128, 32 * 1024), dtype=np.float32)
    for ei in range(128):
        rr, qq = ei // 32, ei % 32
        biasq[32 * rr: 32 * rr + 32, qq] = fb[ei]
        f2q[32 * rr: 32 * rr + 32, qq * 1024:(qq + 1) * 1024] = f2[ei]
    base["ctrl"] = ctrl
    base["f1"] = f1
    base["biasq"] = np.ascontiguousarray(biasq)
    base["f2q"] = np.ascontiguousarray(f2q)
    base["biasq2"] = np.ascontiguousarray(fb.T)
    base["f2q2"] = np.ascontiguousarray(np.transpose(f2, (1, 0, 2)).reshape(32, 128 * 1024))

    in_maps = []
    for i in range(8):
        m = dict(base)
        m["x"] = np.ascontiguousarray(x[i * T:(i + 1) * T])
        in_maps.append(m)
    return in_maps


_DEBUG_STOP = None


def kernel(**inputs):
    in_maps = _host_inputs(inputs)
    nc = build_nc(debug_stop=_DEBUG_STOP)
    res = run_bass_kernel_spmd(nc, in_maps, core_ids=list(range(NCORES)))
    key = "xmid" if _DEBUG_STOP == "attn" else "out"
    out = np.concatenate([np.asarray(r[key]) for r in res.results], axis=0)
    return out.reshape(16, 2048, 1024)
